# version 01
# baseline (speedup 1.0000x reference)
import numpy as np
from contextlib import ExitStack

import concourse.bass as bass
import concourse.mybir as mybir
from concourse.ap import AP
from concourse.bass_utils import run_bass_kernel_spmd

F32 = mybir.dt.float32
F32R = mybir.dt.float32r
BF16 = mybir.dt.bfloat16
AF = mybir.ActivationFunctionType
ALU = mybir.AluOpType
AX = mybir.AxisListType.X

S = 2048
D = 1024
NT = S // 128
DC = D // 128
INNER = 512
NHEAD = 16
HD = 64
F_MOE = 4096
E_MOE = 8
EPS_LN = 1e-5
EPS_FFN = 1e-6
EPS_ATT = 1e-12
EPS_L1F = 1e-5
CSHIFT = 12.0

(ST_MU0, ST_RSTD0, ST_A2, ST_B2, ST_MU1, ST_RSTD1, ST_T0, ST_T1, ST_C1, ST_C2,
 ST_T2, ST_T3, ST_U0, ST_U1) = range(14)


class Bld:
    def __init__(self):
        self.nc = bass.Bass("TRN2", target_bir_lowering=False)
        self.es = ExitStack()
        self.sem = {}
        self.cnt = {}
        self.dram = {}
        self.input_names = []
        self.T = {}

    def get_sem(self, name):
        if name not in self.sem:
            self.sem[name] = self.es.enter_context(self.nc.semaphore(name))
            self.cnt[name] = 0
        return self.sem[name]

    def inc(self, instr, name, amt=1):
        instr.then_inc(self.get_sem(name), amt)
        self.cnt[name] += amt
        return self.cnt[name]

    def wait(self, eng, name, val=None):
        eng.wait_ge(self.get_sem(name), self.cnt[name] if val is None else val)

    def seminc(self, eng, name, amt=1):
        eng.sem_inc(self.get_sem(name), amt)
        self.cnt[name] += amt

    SB_BASE = 16512
    SB_TOP = 229344

    def sb(self, name, shape, dtype, at=None):
        esz = {F32: 4, F32R: 4, BF16: 2}[dtype]
        nbytes = int(np.prod(shape[1:])) * esz
        nbytes = (nbytes + 31) // 32 * 32
        if at is None:
            if not hasattr(self, "sb_off"):
                self.sb_off = self.SB_BASE
            at = self.sb_off
            self.sb_off = at + nbytes
            assert self.sb_off <= self.SB_TOP, f"SBUF overflow at {name}: {self.sb_off}"
        t = self.nc.alloc_sbuf_tensor_at(name, list(shape), dtype, offset=at)
        self.T[name] = t
        self.off = getattr(self, "off", {})
        self.off[name] = at
        return t

    def ps(self, name, shape, dtype=F32):
        t = self.es.enter_context(self.nc.psum_tensor(name, shape, dtype))
        self.T[name] = t
        return t

    def din(self, name, shape, dtype=F32):
        t = self.nc.dram_tensor(name, list(shape), dtype, kind="ExternalInput")
        self.dram[name] = t
        self.input_names.append(name)
        return t

    def dout(self, name, shape, dtype=F32):
        t = self.nc.dram_tensor(name, list(shape), dtype, kind="ExternalOutput")
        self.dram[name] = t
        return t

    def dint(self, name, shape, dtype=F32):
        t = self.nc.dram_tensor(name, list(shape), dtype, kind="Internal")
        self.dram[name] = t
        return t

    def dma(self, eng, out, in_, sem="dm", **kw):
        self.inc(eng.dma_start(out=out, in_=in_, **kw), sem, 16)


def stcol(stats, blk_idx, t0, n=1):
    return stats[:, blk_idx * NT + t0: blk_idx * NT + t0 + n]


def emit_rstd(scal, dst, var_plus_eps):
    scal.activation(dst, var_plus_eps, AF.Ln)
    scal.drain()
    return scal.activation(dst, dst, AF.Exp, scale=-0.5)



def emit_p0(b):
    nc = b.nc
    stats = b.T["stats"]
    xT = b.T["xT"]
    xstage = b.T["xstage"]
    tmpA = b.T["tmpA"]
    tmpB = b.T["tmpB"]
    sq = b.T["sq"]
    ident = b.T["ident"]
    iota_t = b.T["iota_t"]
    pst0 = b.T["PB"]
    pst1 = b.T["PC"]
    x0 = b.dram["x0"].ap()

    with nc.Block("p0a") as blk:
        @blk.sync
        def _(s):
            x0_pt = AP(b.dram["x0"], 0, [[D, 128], [128 * D, NT], [1, D]])
            b.dma(s, xstage[:, :].rearrange("p (t d) -> p t d", t=NT), x0_pt, "d_x0")
            s.wait_ge(b.get_sem("d_x0"), 16)
            x1_pt = AP(b.dram["x1dram"], 0, [[D, 128], [128 * D, NT], [1, D]])
            b.dma(s, x1_pt, xstage[:, :].rearrange("p (t d) -> p t d", t=NT), "d_x1w")

        @blk.gpsimd
        def _(g):
            g.iota(iota_t[:, :], pattern=[[1, 128]], base=0,
                   channel_multiplier=-1,
                   allow_small_or_imprecise_dtypes=True)
            g.drain()
            b.seminc(g, "c_id", 1)

        @blk.vector
        def _(v):
            b.wait(v, "c_id")
            v.tensor_scalar(ident[:, :], iota_t[:, :], 0.0, None, ALU.is_equal)
            v.drain()
            v.tensor_copy(b.T["identr"][:, :], ident[:, :])
            v.drain()
            b.seminc(v, "c_id", 1)
            b.seminc(v, "c_idr", 1)

    with nc.Block("p0b") as blk:
        @blk.vector
        def _(v):
            v.wait_ge(b.get_sem("d_x0"), 16)
            for t in range(NT):
                v.reduce_sum(stcol(stats, ST_MU0, t), xstage[:, t * D:(t + 1) * D], axis=AX)
            b.seminc(v, "c_v", 1)

        @blk.scalar
        def _(sc):
            sc.wait_ge(b.get_sem("d_x0"), 16)
            last = None
            for t in range(NT):
                last = sc.activation(sq[:, :], xstage[:, t * D:(t + 1) * D],
                                     AF.Square, accum_out=stcol(stats, ST_T1, t))
            b.inc(last, "c_s", 1)

    with nc.Block("p0c") as blk:
        @blk.vector
        def _(v):
            b.wait(v, "c_v")
            b.wait(v, "c_s")
            v.drain()
            mu0 = stcol(stats, ST_MU0, 0, NT)
            ss = stcol(stats, ST_T1, 0, NT)
            t0 = stcol(stats, ST_T0, 0, NT)
            v.tensor_scalar_mul(mu0, mu0, 1.0 / D)
            v.tensor_scalar_mul(ss, ss, 1.0 / D)
            v.drain()
            v.tensor_tensor(t0, mu0, mu0, ALU.mult)
            v.drain()
            v.tensor_sub(ss, ss, t0)
            v.drain()
            v.tensor_scalar_add(ss, ss, float(EPS_LN))
            v.drain()
            b.seminc(v, "c_v2", 1)

        @blk.scalar
        def _(sc):
            b.wait(sc, "c_v2")
            emit_rstd(sc, stcol(stats, ST_RSTD0, 0, NT), stcol(stats, ST_T1, 0, NT))
            sc.drain()
            b.seminc(sc, "c_rstd0", 1)

    with nc.Block("p0d") as blk:
        @blk.vector
        def _(v):
            b.wait(v, "c_rstd0")
            b.wait(v, "c_id")
            for t in range(NT):
                tmp = tmpA if t % 2 == 0 else tmpB
                if t >= 2:
                    v.wait_ge(b.get_sem("c_tr"), t - 1)
                    v.wait_ge(b.get_sem("c_sq2"), 2 * (t - 1))
                v.tensor_scalar(tmp[:, :], xstage[:, t * D:(t + 1) * D],
                                stcol(stats, ST_MU0, t),
                                stcol(stats, ST_RSTD0, t),
                                ALU.subtract, ALU.mult)
                v.drain()
                b.seminc(v, "c_xn", 1)

        @blk.tensor
        def _(te):
            for t in range(NT):
                tmp = tmpA if t % 2 == 0 else tmpB
                pst = pst0 if t % 2 == 0 else pst1
                te.wait_ge(b.get_sem("c_xn"), t + 1)
                if t >= 2:
                    te.wait_ge(b.get_sem("c_ev"), t - 1)
                last = None
                for j in range(DC):
                    last = te.matmul(pst[:, j * 128:(j + 1) * 128],
                                     tmp[:, j * 128:(j + 1) * 128], ident[:, :],
                                     start=True, stop=True, is_transpose=True)
                te.drain()
                b.seminc(te, "c_tr", 1)

        @blk.scalar
        def _(sc):
            xTr = xT[:, :].rearrange("p (c s) -> p c s", c=DC)
            for t in range(NT):
                tmp = tmpA if t % 2 == 0 else tmpB
                pst = pst0 if t % 2 == 0 else pst1
                sc.wait_ge(b.get_sem("c_xn"), t + 1)
                sc.activation(sq[:, :], tmp[:, :], AF.Identity,
                              accum_out=stcol(stats, ST_T0, t))
                b.inc(sc.activation(sq[:, :], tmp[:, :], AF.Square,
                                    accum_out=stcol(stats, ST_T1, t)), "c_sq2", 2)
                sc.wait_ge(b.get_sem("c_tr"), t + 1)
                sc.copy(xTr[:, :, t * 128:(t + 1) * 128],
                        pst[:, :].rearrange("p (c x) -> p c x", c=DC))
                sc.drain()
                b.seminc(sc, "c_ev", 1)

    with nc.Block("p0f") as blk:
        @blk.vector
        def _(v):
            b.wait(v, "c_sq2")
            b.wait(v, "c_ev")
            mu2 = stcol(stats, ST_T0, 0, NT)
            ss2 = stcol(stats, ST_T1, 0, NT)
            a2 = stcol(stats, ST_A2, 0, NT)
            v.tensor_scalar_mul(mu2, mu2, 1.0 / D)
            v.tensor_scalar_mul(ss2, ss2, 1.0 / D)
            v.drain()
            v.tensor_tensor(a2, mu2, mu2, ALU.mult)
            v.drain()
            v.tensor_sub(ss2, ss2, a2)
            v.drain()
            v.tensor_scalar_add(ss2, ss2, float(EPS_FFN))
            v.drain()
            b.seminc(v, "c_var2", 1)

        @blk.scalar
        def _(sc):
            b.wait(sc, "c_var2")
            emit_rstd(sc, stcol(stats, ST_B2, 0, NT), stcol(stats, ST_T1, 0, NT))
            sc.drain()
            b.seminc(sc, "c_r2", 1)

        @blk.vector
        def _(v):
            b.wait(v, "c_r2")
            a2 = stcol(stats, ST_A2, 0, NT)
            b2 = stcol(stats, ST_B2, 0, NT)
            mu2 = stcol(stats, ST_T0, 0, NT)
            t1 = stcol(stats, ST_T1, 0, NT)
            mu0 = stcol(stats, ST_MU0, 0, NT)
            rstd0 = stcol(stats, ST_RSTD0, 0, NT)
            v.reciprocal(stcol(stats, ST_C1, 0, NT), b2)
            v.tensor_copy(stcol(stats, ST_C2, 0, NT), mu2)
            v.tensor_tensor(a2, rstd0, b2, ALU.mult)
            v.tensor_tensor(t1, mu0, rstd0, ALU.mult)
            v.drain()
            v.tensor_add(t1, t1, mu2)
            v.drain()
            v.tensor_tensor(b2, t1, b2, ALU.mult)
            v.drain()
            v.tensor_scalar_mul(b2, b2, -1.0)
            v.drain()
            b.seminc(v, "c_aff", 1)

    return




def emit_gate(b):
    nc = b.nc
    xT, stats = b.T["xT"], b.T["stats"]
    wg, scg, m8 = b.T["wg"], b.T["scg"], b.T["m8"]
    gcol, geq = b.T["gcol"], b.T["geq"]
    cw, cwT = b.T["cw"], b.T["cwT"]
    PA = b.T["PA"]
    psg = PA[:, 0:8]
    psct = PA[:, 1024:1152]
    ident = b.T["ident"]

    with nc.Block("gate_ld") as blk:
        @blk.gpsimd
        def _(g):
            b.dma(g, wg[:, :], b.dram["wg_cols"].ap(), "d_gw")

    with nc.Block("gate") as blk:
        @blk.tensor
        def _(te):
            te.wait_ge(b.get_sem("d_gw"), 16)
            b.wait(te, "c_ev")
            b.wait(te, "c_id")
            for t in range(NT):
                if t >= 1:
                    te.wait_ge(b.get_sem("g_d"), t)
                last = None
                for c in range(DC):
                    last = te.matmul(psg, xT[:, c * S + t * 128: c * S + (t + 1) * 128],
                                     wg[:, c * 8:(c + 1) * 8],
                                     start=(c == 0), stop=(c == DC - 1))
                te.drain()
                b.seminc(te, "g_mm", 1)
            for t in range(NT):
                te.wait_ge(b.get_sem("g_cw"), t + 1)
                if t >= 1:
                    te.wait_ge(b.get_sem("g_ct"), t)
                te.matmul(psct[0:8, :], cw[:, t * 8:(t + 1) * 8], ident[:, :],
                          start=True, stop=True)
                te.drain()
                b.seminc(te, "g_tr", 1)

        @blk.vector
        def _(v):
            for t in range(NT):
                v.wait_ge(b.get_sem("g_mm"), t + 1)
                v.tensor_copy(scg[:, :], psg)
                v.drain()
                v.max(out=m8[:, :], in_=scg[:, :])
                v.drain()
                v.tensor_tensor(gcol[:, 0:1], m8[:, 0:1], m8[:, 1:2], ALU.subtract)
                v.drain()
                b.seminc(v, "g_d", 1)
                v.wait_ge(b.get_sem("g_sig"), t + 1)
                v.tensor_scalar(gcol[:, 2:3], gcol[:, 1:2], -1.0, 1.0, ALU.mult, ALU.add)
                v.tensor_scalar(geq[:, 0:8], scg[:, :], m8[:, 0:1], None, ALU.is_equal)
                v.tensor_scalar(geq[:, 8:16], scg[:, :], m8[:, 1:2], None, ALU.is_equal)
                v.drain()
                cwt = cw[:, t * 8:(t + 1) * 8]
                v.tensor_scalar(cwt, geq[:, 0:8], gcol[:, 1:2], None, ALU.mult)
                v.drain()
                v.scalar_tensor_tensor(cwt, geq[:, 8:16], gcol[:, 2:3], cwt,
                                       ALU.mult, ALU.add)
                v.drain()
                b.seminc(v, "g_cw", 1)
            for t in range(NT):
                v.wait_ge(b.get_sem("g_tr"), t + 1)
                v.tensor_copy(cwT[:, t * 128:(t + 1) * 128], psct[0:8, :])
                v.drain()
                b.seminc(v, "g_ct", 1)

        @blk.scalar
        def _(sc):
            for t in range(NT):
                sc.wait_ge(b.get_sem("g_d"), t + 1)
                sc.activation(gcol[:, 1:2], gcol[:, 0:1], AF.Sigmoid)
                sc.drain()
                b.seminc(sc, "g_sig", 1)



def emit_x0_reload(b):
    nc = b.nc
    with nc.Block("x0rl") as blk:
        @blk.sync
        def _(s):
            s.wait_ge(b.get_sem("a_out"), b.cnt["a_out"])
            x0_pt = AP(b.dram["x0"], 0, [[D, 128], [128 * D, NT], [1, D]])
            b.dma(s, b.T["xstage"][:, :].rearrange("p (t d) -> p t d", t=NT),
                  x0_pt, "d_x0b")


def emit_rebuild_xhatT(b, wait_reload=False):
    nc = b.nc
    stats, xT, xstage = b.T["stats"], b.T["xT"], b.T["xstage"]
    tmpA, tmpB = b.T["tmpA"], b.T["tmpB"]
    PB, PC = b.T["PB"], b.T["PC"]
    ident, identr = b.T["ident"], b.T["identr"]
    with nc.Block("rebuild") as blk:
        @blk.vector
        def _(v):
            b.wait(v, "c_aff")
            b.wait(v, "c_id")
            if wait_reload:
                v.wait_ge(b.get_sem("d_x0b"), 16)
            for t in range(NT):
                tmp = tmpA if t % 2 == 0 else tmpB
                if t >= 2:
                    v.wait_ge(b.get_sem("r_tr"), t - 1)
                v.tensor_scalar(tmp[:, :], xstage[:, t * D:(t + 1) * D],
                                stcol(stats, ST_A2, t), stcol(stats, ST_B2, t),
                                ALU.mult, ALU.add)
                v.drain()
                b.seminc(v, "r_xn", 1)

        @blk.tensor
        def _(te):
            b.wait(te, "g_tr")
            for t in range(NT):
                tmp = tmpA if t % 2 == 0 else tmpB
                pst = PB if t % 2 == 0 else PC
                te.wait_ge(b.get_sem("r_xn"), t + 1)
                if t >= 2:
                    te.wait_ge(b.get_sem("r_ev"), t - 1)
                last = None
                for j in range(DC):
                    last = te.matmul(pst[:, j * 128:(j + 1) * 128],
                                     tmp[:, j * 128:(j + 1) * 128], ident[:, :],
                                     start=True, stop=True, is_transpose=True)
                te.drain()
                b.seminc(te, "r_tr", 1)

        @blk.scalar
        def _(sc):
            xTr = xT[:, :].rearrange("p (c s) -> p c s", c=DC)
            for t in range(NT):
                pst = PB if t % 2 == 0 else PC
                sc.wait_ge(b.get_sem("r_tr"), t + 1)
                sc.copy(xTr[:, :, t * 128:(t + 1) * 128],
                        pst[:, :].rearrange("p (c x) -> p c x", c=DC))
                sc.drain()
                b.seminc(sc, "r_ev", 1)



def emit_moe(b, dest, accumulate):
    nc = b.nc
    stats, xT = b.T["stats"], b.T["xT"]
    cw, cwT = b.T["cw"], b.T["cwT"]
    h_stage = b.T["xstage"][:, :].bitcast(BF16)
    y_acc = b.T["y_acc"]
    w1sl = [b.T["w1A"], b.T["w1B"]]
    w2sl = [b.T["w2A"], b.T["w2B"]]
    b1c, gamc, betc = b.T["b1c"], b.T["gamc"], b.T["betc"]
    hb, b2s, sq = b.T["hb"], b.T["b2s"], b.T["sq"]
    PA, PB, PC = b.T["PA"], b.T["PB"], b.T["PC"]
    identr = b.T["identr"]
    ytiles = [PA[:, i * 512:(i + 1) * 512] for i in range(4)] + \
             [PB[:, 0:512], PB[:, 512:1024], PC[:, 0:512], PC[:, 512:1024]]
    psb = PA[:, 0:64]
    x1d = b.dram[dest]
    NTB, TBT = 2, 8

    with nc.Block("moe_ld") as blk:
        @blk.gpsimd
        def _(g):
            if accumulate:
                g.wait_ge(b.get_sem("r_ev"), NT)
            b.dma(g, b1c[:, :], b.dram["b1_cols"].ap(), "d_mc")
            b.dma(g, gamc[:, :], b.dram["gam_cols"].ap(), "d_mc")
            b.dma(g, betc[:, 0:E_MOE * DC], b.dram["bet_cols"].ap(), "d_mc")
            b.dma(g, b2s[:, :], b.dram["L0_ffn_b2"].ap(), "d_mc")

    with nc.Block("moe") as blk:
        @blk.gpsimd
        def _(g):
            b.wait(g, "r_ev")
            w1h, w2h = b.dram["L0_ffn_W1"], b.dram["L0_ffn_W2"]
            for tb in range(NTB):
                for e in range(E_MOE):
                    for fb in range(16):
                        k = (tb * E_MOE + e) * 16 + fb
                        if k >= 2:
                            g.wait_ge(b.get_sem("m_hmm"), 2 * (k - 1))
                        src = AP(w1h, e * D * F_MOE + fb * 256,
                                 [[F_MOE, 128], [128 * F_MOE, DC], [1, 256]])
                        b.dma(g, w1sl[k % 2][:, :].rearrange("p (c f) -> p c f", c=DC),
                              src, "d_w1")
                    for dh in range(2):
                        for fb in range(32):
                            k2 = ((tb * E_MOE + e) * 2 + dh) * 32 + fb
                            if k2 >= 2:
                                prev = k2 - 2
                                if prev % 32 == 31:
                                    g.wait_ge(b.get_sem("m_ymm"), prev // 32 + 1)
                                else:
                                    g.wait_ge(b.get_sem("m_w2f"),
                                              prev + 1 - prev // 32)
                            src = AP(w2h, e * F_MOE * D + fb * 128 * D + dh * 512,
                                     [[D, 128], [1, 512]])
                            b.dma(g, w2sl[k2 % 2][:, :], src, "d_w2")
                if accumulate:
                    for tt in range(TBT):
                        jj = tb * TBT + tt
                        g.wait_ge(b.get_sem("m_fin"), jj + 1)
                        gt = tb * TBT + tt
                        dstap = AP(x1d, gt * 128 * D, [[D, 128], [1, D]])
                        b.dma(g, dstap, y_acc[:, tt * 1024:(tt + 1) * 1024],
                              "d_yout", accum_op=ALU.add)

        @blk.vector
        def _(v):
            for tb in range(NTB):
                for e in range(E_MOE):
                    for fb in range(16):
                        k = (tb * E_MOE + e) * 16 + fb
                        v.wait_ge(b.get_sem("d_w1"), 16 * (k + 1))
                        sl = w1sl[k % 2][:, :].rearrange("p (c f) -> p c f", c=DC)
                        gsl = gamc[:, e * DC:(e + 1) * DC].unsqueeze(2).to_broadcast([128, DC, 256])
                        v.tensor_tensor(sl, sl, gsl, ALU.mult)
                        v.drain()
                        b.seminc(v, "m_w1s", 1)
                        if tb == 0:
                            v.wait_ge(b.get_sem("m_bmm"), e * 16 + fb + 1)
                            v.tensor_add(hb[:, e * 32 + fb * 2: e * 32 + fb * 2 + 1],
                                         psb[:, fb * 4:fb * 4 + 1],
                                         b1c[:, e * 32 + fb * 2:e * 32 + fb * 2 + 1])
                            v.tensor_add(hb[:, e * 32 + fb * 2 + 1: e * 32 + fb * 2 + 2],
                                         psb[:, fb * 4 + 2:fb * 4 + 3],
                                         b1c[:, e * 32 + fb * 2 + 1:e * 32 + fb * 2 + 2])
                            v.drain()
                            b.seminc(v, "m_hb", 1)
                    for dh in range(2):
                        for tt in range(TBT):
                            v.wait_ge(b.get_sem("m_ymm"), (tb * E_MOE + e) * 2 + dh + 1)
                            if tb == 1 and e == 0:
                                v.wait_ge(b.get_sem("d_yout" if accumulate else "d_ydev"),
                                          16 * TBT)
                            ya = y_acc[:, tt * 1024 + dh * 512: tt * 1024 + dh * 512 + 512]
                            gt = tb * TBT + tt
                            cwc = cw[:, gt * 8 + e: gt * 8 + e + 1]
                            if e == 0:
                                op = v.tensor_scalar(ya, ytiles[tt], cwc, None, ALU.mult)
                            else:
                                op = v.scalar_tensor_tensor(ya, ytiles[tt], cwc, ya,
                                                            ALU.mult, ALU.add)
                            op
                            v.drain()
                            b.seminc(v, "m_yev", 1)
                for tt in range(TBT):
                    jj = tb * TBT + tt
                    gt = tb * TBT + tt
                    v.wait_ge(b.get_sem("m_imm"), jj + 1)
                    v.wait_ge(b.get_sem("m_bm2"), jj + 1)
                    ya = y_acc[:, tt * 1024:(tt + 1) * 1024]
                    psi = PB[:, :].bitcast(F32R)
                    v.tensor_scalar(sq[:, :], psi, stcol(stats, ST_C1, gt),
                                    stcol(stats, ST_C2, gt), ALU.mult, ALU.add)
                    v.drain()
                    v.tensor_add(sq[:, :], sq[:, :], PC[:, :])
                    v.drain()
                    v.tensor_add(ya, ya, sq[:, :])
                    v.drain()
                    b.seminc(v, "m_fin", 1)

        @blk.tensor
        def _(te):
            for tb in range(NTB):
                if tb == 1:
                    b.wait(te, "m_fin", TBT)
                for e in range(E_MOE):
                    for fb in range(16):
                        k = (tb * E_MOE + e) * 16 + fb
                        te.wait_ge(b.get_sem("m_w1s"), k + 1)
                        sl = w1sl[k % 2]
                        if fb == 0 and (tb + e) > 0:
                            te.wait_ge(b.get_sem("m_yev"),
                                       (tb * E_MOE + e) * 2 * TBT)
                        if tb == 0:
                            te.wait_ge(b.get_sem("m_hb"), e * 16 + fb)
                            last = None
                            for fsub in range(2):
                                for c in range(DC):
                                    last = te.matmul(
                                        psb[:, (fb * 2 + fsub) * 2:
                                            (fb * 2 + fsub) * 2 + 2],
                                        sl[:, c * 256 + fsub * 128:
                                           c * 256 + fsub * 128 + 128],
                                        betc[:, e * DC + c: e * DC + c + 2],
                                        start=(c == 0), stop=(c == DC - 1))
                            te.drain()
                            b.seminc(te, "m_bmm", 1)
                        for fsub in range(2):
                            hpi = k * 2 + fsub
                            ph = PB if fsub % 2 == 0 else PC
                            if hpi >= 2:
                                te.wait_ge(b.get_sem("m_hev"), hpi - 1)
                            last = None
                            for c in range(DC):
                                for nh in range(2):
                                    last = te.matmul(
                                        ph[:, nh * 512:(nh + 1) * 512],
                                        sl[:, c * 256 + fsub * 128:
                                           c * 256 + fsub * 128 + 128],
                                        xT[:, c * S + tb * 1024 + nh * 512:
                                           c * S + tb * 1024 + (nh + 1) * 512],
                                        start=(c == 0), stop=(c == DC - 1))
                            te.drain()
                            b.seminc(te, "m_hmm", 1)
                    for dh in range(2):
                        for fb in range(32):
                            k2 = ((tb * E_MOE + e) * 2 + dh) * 32 + fb
                            te.wait_ge(b.get_sem("d_w2"), 16 * (k2 + 1))
                            if fb == 0 and dh == 0:
                                te.wait_ge(b.get_sem("m_hev"),
                                           (tb * E_MOE + e) * 32 + 32)
                            if fb == 0 and dh == 1:
                                te.wait_ge(b.get_sem("m_yev"),
                                           (tb * E_MOE + e) * 2 * TBT + TBT)
                                if tb * E_MOE + e >= 1:
                                    te.wait_ge(b.get_sem("m_yev"),
                                               (tb * E_MOE + e) * 2 * TBT)
                                if tb == 1 and e == 0:
                                    te.wait_ge(b.get_sem("m_fin"), TBT)
                            for tt in range(TBT):
                                last = te.matmul(
                                    ytiles[tt],
                                    h_stage[:, fb * 1024 + tt * 128:
                                            fb * 1024 + (tt + 1) * 128],
                                    w2sl[k2 % 2][:, :],
                                    start=(fb == 0), stop=(fb == 31))
                            if fb == 31:
                                te.drain()
                                b.seminc(te, "m_ymm", 1)
                            else:
                                te.drain()
                                b.seminc(te, "m_w2f", 1)
                for tt in range(TBT):
                    jj = tb * TBT + tt
                    gt = tb * TBT + tt
                    if tt == 0:
                        te.wait_ge(b.get_sem("m_yev"), (tb + 1) * E_MOE * 2 * TBT)
                    if jj >= 1:
                        te.wait_ge(b.get_sem("m_fin"), jj)
                    psi = PB[:, :].bitcast(F32R)
                    last = None
                    for j in range(DC):
                        last = te.matmul(psi[:, j * 128:(j + 1) * 128],
                                         xT[:, j * S + gt * 128:j * S + (gt + 1) * 128],
                                         identr[:, :], start=True, stop=True,
                                         is_transpose=True)
                    te.drain()
                    b.seminc(te, "m_imm", 1)
                    last = None
                    for nh in range(2):
                        last = te.matmul(PC[:, nh * 512:(nh + 1) * 512],
                                         cwT[:, gt * 128:(gt + 1) * 128],
                                         b2s[:, nh * 512:(nh + 1) * 512],
                                         start=True, stop=True)
                    te.drain()
                    b.seminc(te, "m_bm2", 1)

        @blk.scalar
        def _(sc):
            for tb in range(NTB):
                for e in range(E_MOE):
                    for fb in range(16):
                        k = (tb * E_MOE + e) * 16 + fb
                        for fsub in range(2):
                            hpi = k * 2 + fsub
                            sc.wait_ge(b.get_sem("m_hmm"), hpi + 1)
                            if tb == 0:
                                sc.wait_ge(b.get_sem("m_hb"), e * 16 + fb + 1)
                            ph = PB if fsub % 2 == 0 else PC
                            blkidx = fb * 2 + fsub
                            dst = h_stage[:, blkidx * 1024 + 0:
                                          blkidx * 1024 + 1024]
                            sc.activation(dst, ph[:, :], AF.Relu,
                                          bias=hb[:, e * 32 + blkidx:
                                                  e * 32 + blkidx + 1])
                            sc.drain()
                            b.seminc(sc, "m_hev", 1)

        if not accumulate:
            @blk.sync
            def _(s):
                for tb in range(NTB):
                    for tt in range(TBT):
                        jj = tb * TBT + tt
                        s.wait_ge(b.get_sem("m_fin"), jj + 1)
                        gt = tb * TBT + tt
                        dstap = AP(x1d, gt * 128 * D, [[D, 128], [1, D]])
                        b.dma(s, dstap, y_acc[:, tt * 1024:(tt + 1) * 1024], "d_ydev")


def emit_stats_zT(b, src_name, gains=True):
    nc = b.nc
    stats, xT, xstage = b.T["stats"], b.T["xT"], b.T["xstage"]
    tmpA, tmpB, sq = b.T["tmpA"], b.T["tmpB"], b.T["sq"]
    PB, PC = b.T["PB"], b.T["PC"]
    ident = b.T["ident"]
    lng = b.T["lng"] if gains else None
    lnb = b.T["lnb"] if gains else None

    with nc.Block("zt_load") as blk:
        @blk.sync
        def _(s):
            src_pt = AP(b.dram[src_name], 0, [[D, 128], [128 * D, NT], [1, D]])
            b.dma(s, xstage[:, :].rearrange("p (t d) -> p t d", t=NT), src_pt, "z_dx")

    with nc.Block("zt_stats") as blk:
        @blk.vector
        def _(v):
            v.wait_ge(b.get_sem("z_dx"), b.cnt["z_dx"])
            for t in range(NT):
                v.reduce_sum(stcol(stats, ST_MU1, t), xstage[:, t * D:(t + 1) * D], axis=AX)
            b.seminc(v, "z_sums", 1)

        @blk.scalar
        def _(sc):
            sc.wait_ge(b.get_sem("z_dx"), b.cnt["z_dx"])
            last = None
            for t in range(NT):
                last = sc.activation(sq[:, :], xstage[:, t * D:(t + 1) * D],
                                     AF.Square, accum_out=stcol(stats, ST_T1, t))
            b.inc(last, "z_ss", 1)

    with nc.Block("zt_statfin") as blk:
        @blk.vector
        def _(v):
            b.wait(v, "z_sums")
            b.wait(v, "z_ss")
            v.drain()
            mu = stcol(stats, ST_MU1, 0, NT)
            ss = stcol(stats, ST_T1, 0, NT)
            t0 = stcol(stats, ST_T0, 0, NT)
            v.tensor_scalar_mul(mu, mu, 1.0 / D)
            v.tensor_scalar_mul(ss, ss, 1.0 / D)
            v.drain()
            v.tensor_tensor(t0, mu, mu, ALU.mult)
            v.drain()
            v.tensor_sub(ss, ss, t0)
            v.drain()
            v.tensor_scalar_add(ss, ss, float(EPS_LN))
            v.drain()
            b.seminc(v, "z_var", 1)

        @blk.scalar
        def _(sc):
            b.wait(sc, "z_var")
            emit_rstd(sc, stcol(stats, ST_RSTD1, 0, NT), stcol(stats, ST_T1, 0, NT))
            sc.drain()
            b.seminc(sc, "z_rstd", 1)

    with nc.Block("zt_main") as blk:
        @blk.vector
        def _(v):
            b.wait(v, "z_rstd")
            b.wait(v, "c_id")
            for t in range(NT):
                tmp = tmpA if t % 2 == 0 else tmpB
                if t >= 2:
                    v.wait_ge(b.get_sem("z_tr"), t - 1)
                v.tensor_scalar(tmp[:, :], xstage[:, t * D:(t + 1) * D],
                                stcol(stats, ST_MU1, t),
                                stcol(stats, ST_RSTD1, t),
                                ALU.subtract, ALU.mult)
                v.drain()
                b.seminc(v, "z_xn", 1)

        @blk.tensor
        def _(te):
            for t in range(NT):
                tmp = tmpA if t % 2 == 0 else tmpB
                pst = PB if t % 2 == 0 else PC
                te.wait_ge(b.get_sem("z_xn"), t + 1)
                if t >= 2:
                    te.wait_ge(b.get_sem("z_ev"), DC * (t - 1))
                last = None
                for j in range(DC):
                    last = te.matmul(pst[:, j * 128:(j + 1) * 128],
                                     tmp[:, j * 128:(j + 1) * 128], ident[:, :],
                                     start=True, stop=True, is_transpose=True)
                te.drain()
                b.seminc(te, "z_tr", 1)

        @blk.scalar
        def _(sc):
            for t in range(NT):
                pst = PB if t % 2 == 0 else PC
                sc.wait_ge(b.get_sem("z_tr"), t + 1)
                for j in range(DC):
                    if gains:
                        op = sc.activation(
                            xT[:, j * S + t * 128: j * S + (t + 1) * 128],
                            pst[:, j * 128:(j + 1) * 128], AF.Identity,
                            bias=lnb[:, j:j + 1], scale=lng[:, j:j + 1])
                    else:
                        op = sc.activation(
                            xT[:, j * S + t * 128: j * S + (t + 1) * 128],
                            pst[:, j * 128:(j + 1) * 128], AF.Identity)
                    b.inc(op, "z_ev", 1)
            sc.drain()
            b.seminc(sc, "z_done", 1)



def emit_l1ffn(b, dest, mode="dev"):
    nc = b.nc
    stats, xT = b.T["stats"], b.T["xT"]
    PA, PB, PC = b.T["PA"], b.T["PB"], b.T["PC"]
    zsum = b.T["zsum"]
    bnst = b.T["bnst"]
    wr, brrow = b.T["wr"], b.T["brrow"]
    prow = b.T["prow"]
    bcast = b.T["bcast"]
    onescol = b.T["onescol"]
    ohcol = b.T["ohcol"]
    wdn, wup = b.T["wdn"], b.T["wup"]
    bdT, bup = b.T["bdT"], b.T["bup"]
    wdsel = b.T["wdsel"]
    wupsel = b.T["wupsel"]
    bdsel = b.T["bdsel"]
    hdnT = b.T["hdnT"]
    ohdram = b.dram["ohdram"]
    x1d = b.dram[dest]

    with nc.Block("l1f_ld") as blk:
        @blk.vector
        def _(v):
            xs = b.T["xstage"]
            v.memset(xs[0:4, 0:S], 0.0)
            v.drain()
            v.memset(xs[0:1, 0:S], 1.0)
            v.memset(onescol[0:1, :], 1.0)
            v.drain()
            b.seminc(v, "f_ones", 1)

        @blk.gpsimd
        def _(g):
            b.dma(g, b.T["lng"][:, :], b.dram["lng_cols"].ap(), "f_mc")
            b.dma(g, b.T["lnb"][:, :], b.dram["lnb_cols"].ap(), "f_mc")
            b.dma(g, bnst[:, :], b.dram["bn_cols"].ap(), "f_mc")
            b.dma(g, wr[:, :], b.dram["wr_cols"].ap(), "f_mc")
            b.dma(g, brrow[:, :], b.dram["br_row"].ap(), "f_mc")
            b.dma(g, wdn[:, :], b.dram["wdn_cols"].ap(), "f_mc")
            b.dma(g, wup[:, :], b.dram["wup_rows"].ap(), "f_mc")
            b.dma(g, bdT[:, :], b.dram["b_downT"].ap(), "f_mc")
            b.dma(g, bup[:, :], b.dram["L1_ffn_b_up"].ap(), "f_mc")
            b.wait(g, "f_ones")
            b.dma(g, b.T["hdnT"][64:66, :], b.T["xstage"][0:2, 0:S], "f_pad")
            b.dma(g, b.T["wupsel"][64:66, :], b.T["xstage"][1:3, 0:1024], "f_pad")

    with nc.Block("l1f_gate") as blk:
        @blk.vector
        def _(v):
            b.wait(v, "z_done")
            v.wait_ge(b.get_sem("f_mc"), b.cnt["f_mc"])
            for c in range(DC):
                v.reduce_sum(zsum[:, c:c + 1], xT[:, c * S:(c + 1) * S], axis=AX)
            v.drain()
            v.tensor_scalar_mul(zsum[:, :], zsum[:, :], 1.0 / S)
            v.drain()
            v.tensor_sub(zsum[:, :], zsum[:, :], bnst[:, 2 * DC:3 * DC])
            v.tensor_scalar_add(b.T["sq"][:, 0:DC], bnst[:, 3 * DC:4 * DC],
                                float(EPS_L1F))
            v.drain()
            b.seminc(v, "f_bn1", 1)

        @blk.scalar
        def _(sc):
            b.wait(sc, "f_bn1")
            sc.drain()
            emit_rstd(sc, b.T["sq"][:, DC:2 * DC], b.T["sq"][:, 0:DC])
            sc.drain()
            b.seminc(sc, "f_bn2", 1)

        @blk.vector
        def _(v):
            b.wait(v, "f_bn2")
            v.drain()
            v.tensor_tensor(zsum[:, :], zsum[:, :], b.T["sq"][:, DC:2 * DC], ALU.mult)
            v.drain()
            v.tensor_tensor(zsum[:, :], zsum[:, :], bnst[:, 0:DC], ALU.mult)
            v.drain()
            v.tensor_add(zsum[:, :], zsum[:, :], bnst[:, DC:2 * DC])
            v.drain()
            b.seminc(v, "f_zbn", 1)

        @blk.tensor
        def _(te):
            b.wait(te, "f_zbn")
            last = None
            for c in range(DC):
                last = te.matmul(PA[0:1, 0:8], zsum[:, c:c + 1], wr[:, c * 8:(c + 1) * 8],
                                 start=(c == 0), stop=(c == DC - 1))
            te.drain()
            b.seminc(te, "f_lmm", 1)

        @blk.vector
        def _(v):
            b.wait(v, "f_lmm")
            v.tensor_add(prow[0:1, 0:8], PA[0:1, 0:8], brrow[0:1, 0:8])
            v.drain()
            v.reduce_max(prow[0:1, 8:9], prow[0:1, 0:8], axis=AX)
            v.drain()
            v.tensor_scalar(prow[0:1, 0:8], prow[0:1, 0:8], prow[0:1, 8:9],
                            None, ALU.subtract)
            v.drain()
            b.seminc(v, "f_sm1", 1)

        @blk.scalar
        def _(sc):
            b.wait(sc, "f_sm1")
            sc.drain()
            sc.activation(prow[0:1, 0:8], prow[0:1, 0:8], AF.Exp)
            sc.drain()
            b.seminc(sc, "f_sm2", 1)

        @blk.vector
        def _(v):
            b.wait(v, "f_sm2")
            v.drain()
            v.reduce_sum(prow[0:1, 8:9], prow[0:1, 0:8], axis=AX)
            v.drain()
            v.reciprocal(prow[0:1, 9:10], prow[0:1, 8:9])
            v.drain()
            v.tensor_scalar(prow[0:1, 0:8], prow[0:1, 0:8], prow[0:1, 9:10],
                            None, ALU.mult)
            v.drain()
            v.reduce_max(prow[0:1, 8:9], prow[0:1, 0:8], axis=AX)
            v.drain()
            v.tensor_scalar(prow[0:1, 0:8], prow[0:1, 0:8], prow[0:1, 8:9],
                            None, ALU.is_equal)
            v.drain()
            b.seminc(v, "f_oh", 1)

        @blk.sync
        def _(s):
            b.wait(s, "f_oh")
            b.dma(s, ohdram.ap(), prow[0:1, 0:16], "f_ohd")
            s.wait_ge(b.get_sem("f_ohd"), b.cnt["f_ohd"])
            b.dma(s, ohcol[:, 0], ohdram.ap(), "f_ohc")

        @blk.tensor
        def _(te):
            te.wait_ge(b.get_sem("f_ohc"), b.cnt["f_ohc"])
            b.wait(te, "f_ones")
            te.matmul(PA[:, 32:48], onescol[0:1, :], prow[0:1, 0:16], start=True, stop=True)
            te.drain()
            b.seminc(te, "f_bc1", 1)
            last = None
            for nh in range(2):
                last = te.matmul(PB[0:1, nh * 512:(nh + 1) * 512], ohcol[0:8, 0:1],
                                 bup[:, nh * 512:(nh + 1) * 512],
                                 start=True, stop=True)
            te.drain()
            b.seminc(te, "f_bup", 1)

        @blk.vector
        def _(v):
            b.wait(v, "f_bc1")
            v.tensor_copy(bcast[:, :], PA[:, 32:48])
            v.drain()
            b.seminc(v, "f_bc2", 1)
            b.wait(v, "f_bup")
            v.drain()
            v.tensor_copy(b.T["wupsel"][64:65, :], PB[0:1, 0:1024])
            v.drain()
            b.seminc(v, "f_bups", 1)

    with nc.Block("l1f_sel") as blk:
        @blk.vector
        def _(v):
            b.wait(v, "f_bc2")
            v.drain()
            for e in range(E_MOE):
                sl = wdn[:, e * DC * 64:(e + 1) * DC * 64]
                if e == 0:
                    v.tensor_scalar(wdsel[:, :], sl, bcast[:, 0:1], None, ALU.mult)
                else:
                    v.scalar_tensor_tensor(wdsel[:, :], sl, bcast[:, e:e + 1],
                                           wdsel[:, :], ALU.mult, ALU.add)
                v.drain()
            b.seminc(v, "f_wds", 1)
            for e in range(E_MOE):
                sl = wup[0:64, e * 1024:(e + 1) * 1024]
                if e == 0:
                    v.tensor_scalar(wupsel[0:64, :], sl, bcast[0:64, 0:1], None, ALU.mult)
                else:
                    v.scalar_tensor_tensor(wupsel[0:64, :], sl, bcast[0:64, e:e + 1],
                                           wupsel[0:64, :], ALU.mult, ALU.add)
                v.drain()
            b.seminc(v, "f_wus", 1)
            v.tensor_tensor(bdsel[:, 0:8], bdT[:, :], bcast[0:64, 0:8], ALU.mult)
            v.drain()
            v.reduce_sum(bdsel[:, 0:1], bdsel[:, 0:8], axis=AX)
            v.drain()
            b.seminc(v, "f_bds", 1)


    with nc.Block("l1f_hdn") as blk:
        @blk.tensor
        def _(te):
            b.wait(te, "f_wds")
            for nchunk in range(4):
                ph = PB[0:64, 0:512] if nchunk % 2 == 0 else PC[0:64, 0:512]
                if nchunk >= 2:
                    te.wait_ge(b.get_sem("f_hev"), nchunk - 1)
                last = None
                for c in range(DC):
                    last = te.matmul(ph, wdsel[:, c * 64:(c + 1) * 64],
                                     xT[:, c * S + nchunk * 512: c * S + (nchunk + 1) * 512],
                                     start=(c == 0), stop=(c == DC - 1))
                te.drain()
                b.seminc(te, "f_hmm", 1)

        @blk.scalar
        def _(sc):
            for nchunk in range(4):
                ph = PB[0:64, 0:512] if nchunk % 2 == 0 else PC[0:64, 0:512]
                sc.wait_ge(b.get_sem("f_hmm"), nchunk + 1)
                if nchunk == 0:
                    b.wait(sc, "f_bds")
                sc.activation(hdnT[0:64, nchunk * 512:(nchunk + 1) * 512],
                              ph, AF.Relu, bias=bdsel[:, 0:1])
                sc.drain()
                b.seminc(sc, "f_hev", 1)

    with nc.Block("l1f_y") as blk:
        @blk.tensor
        def _(te):
            b.wait(te, "f_wus")
            b.wait(te, "f_bups")
            te.wait_ge(b.get_sem("f_pad"), 32)
            for t in range(NT):
                py = PA[:, 0:1024] if t % 2 == 0 else PA[:, 1024:2048]
                te.wait_ge(b.get_sem("f_hev"), t // 4 + 1)
                if t >= 2:
                    te.wait_ge(b.get_sem("f_yev"), t - 1)
                last = None
                for nh in range(2):
                    last = te.matmul(py[:, nh * 512:(nh + 1) * 512],
                                     hdnT[:, t * 128:(t + 1) * 128],
                                     wupsel[:, nh * 512:(nh + 1) * 512],
                                     start=True, stop=True)
                te.drain()
                b.seminc(te, "f_ymm", 1)

        @blk.vector
        def _(v):
            for t in range(NT):
                py = PA[:, 0:1024] if t % 2 == 0 else PA[:, 1024:2048]
                v.wait_ge(b.get_sem("f_ymm"), t + 1)
                if t >= 2:
                    v.wait_ge(b.get_sem("f_out"), 16 * (t - 1))
                op = v.tensor_scalar(b.T["youts"][:, (t % 2) * D:(t % 2) * D + D],
                                     py, bcast[:, 8:9], None, ALU.mult)
                b.inc(op, "f_yev", 1)
                v.drain()
                b.seminc(v, "f_yc", 1)

        @blk.sync
        def _(s):
            for t in range(NT):
                s.wait_ge(b.get_sem("f_yc"), t + 1)
                dstap = AP(x1d, t * 128 * D, [[D, 128], [1, D]])
                b.dma(s, dstap, b.T["youts"][:, (t % 2) * D:(t % 2) * D + D], "f_out")
            s.wait_ge(b.get_sem("f_out"), b.cnt["f_out"])




def emit_l0attn(b, dest, accumulate, wait_gate=False):
    nc = b.nc
    stats, xT = b.T["stats"], b.T["xT"]
    PA, PB, PC = b.T["PA"], b.T["PB"], b.T["PC"]
    identb, identr = b.T["identb"], b.T["identr"]
    distT, distTr = b.T["distT"], b.T["distTr"]
    wq, wk, wv = b.T["wq"], b.T["wk"], b.T["wv"]
    qT, kT, vT = b.T["qT"], b.T["kT"], b.T["vT"]
    vg = b.T["vg"]
    qd_sb = b.T["qd_sb"]
    relq, relk, expS = b.T["relq"], b.T["relk"], b.T["expS"]
    ctx = b.T["ctx"]
    rrow, rbc = b.T["rrow"], b.T["rbc"]
    bqc, bkc, bvc = b.T["bqc"], b.T["bkc"], b.T["bvc"]
    cshift = b.T["cshift"]
    wo, gbc = b.T["wo"], b.T["gbc"]
    onescol2 = b.T["onescol2"]
    tmpA, tmpB, sq = b.T["tmpA"], b.T["tmpB"], b.T["sq"]
    qdd, kdd = b.dram["qdrev_dram"], b.dram["kd_dram"]
    outd = b.dram[dest]

    with nc.Block("a_setup") as blk:
        @blk.gpsimd
        def _(g):
            b.dma(g, distT[:, :], b.dram["dist_padT"].ap(), "a_dc")
            b.dma(g, distTr[:, :], b.dram["dist_padT_rev"].ap(), "a_dc")
            b.dma(g, bqc[:, :], b.dram["bq_cols"].ap(), "a_dc")
            b.dma(g, bkc[:, :], b.dram["bk_cols"].ap(), "a_dc")
            b.dma(g, bvc[:, :], b.dram["bv_cols"].ap(), "a_dc")
            b.dma(g, wo[:, :], b.dram["wo_rows"].ap(), "a_dc")

        @blk.vector
        def _(v):
            b.wait(v, "c_id")
            v.tensor_copy(identb[:, :], b.T["ident"][:, :])
            v.memset(cshift[:, :], -float(CSHIFT))
            v.memset(onescol2[0:1, :], 1.0)
            vr = vg[:, :].rearrange("p (t w) -> p t w", w=130)
            v.memset(vr[:, :, 64:65], 1.0)
            v.memset(vr[:, :, 129:130], 1.0)
            v.drain()
            b.seminc(v, "a_vset", 1)


    for g in range(8):
        with nc.Block(f"a_g{g}") as blk:
            @blk.gpsimd
            def _(gg):
                if g >= 1:
                    gg.wait_ge(b.get_sem("a_pjm"), 12 * g)
                for nm, wt in (("L0_attn_Wq", wq), ("L0_attn_Wk", wk),
                               ("L0_attn_Wv", wv)):
                    src = AP(b.dram[nm], g * 128,
                             [[D, 128], [128 * D, DC], [1, 128]])
                    b.dma(gg, wt[:, :].rearrange("p (c f) -> p c f", c=DC),
                          src, "a_dw")

            @blk.tensor
            def _(te):
                te.wait_ge(b.get_sem("a_dw"), 16 * 3 * (g + 1))
                b.wait(te, "c_ev")
                for which, wt in ((0, wq), (1, wk), (2, wv)):
                    for n4 in range(4):
                        ph = PB if n4 % 2 == 0 else PC
                        pj = (g * 3 + which) * 4 + n4
                        if pj >= 2:
                            te.wait_ge(b.get_sem("a_pje"), pj - 1)
                        for c in range(DC):
                            te.matmul(ph[:, 0:512],
                                      wt[:, c * 128:(c + 1) * 128],
                                      xT[:, c * S + n4 * 512:c * S + (n4 + 1) * 512],
                                      start=(c == 0), stop=(c == DC - 1))
                        te.drain()
                        b.seminc(te, "a_pjm", 1)
                b.wait(te, "a_vset")
                for t in range(NT):
                    vti = g * NT + t
                    te.wait_ge(b.get_sem("a_pje"), 12 * (g + 1))
                    if vti >= 1:
                        te.wait_ge(b.get_sem("a_vge"), vti)
                    te.matmul(PB[:, 0:64].bitcast(BF16),
                              vT[:, t * 128:(t + 1) * 128], identb[:, :],
                              start=True, stop=True, is_transpose=True)
                    te.drain()
                    b.seminc(te, "a_vgm", 1)

            @blk.scalar
            def _(sc):
                for which, dst, bc in ((0, qT, bqc), (1, kT, bkc), (2, vT, bvc)):
                    for n4 in range(4):
                        ph = PB if n4 % 2 == 0 else PC
                        pj = (g * 3 + which) * 4 + n4
                        sc.wait_ge(b.get_sem("a_pjm"), pj + 1)
                        sc.activation(dst[:, n4 * 512:(n4 + 1) * 512], ph[:, 0:512],
                                      AF.Identity, bias=bc[:, g:g + 1])
                        sc.drain()
                        b.seminc(sc, "a_pje", 1)
                for t in range(NT):
                    vti = g * NT + t
                    sc.wait_ge(b.get_sem("a_vgm"), vti + 1)
                    pbb = PB[:, 0:64].bitcast(BF16)
                    sc.copy(vg[:, t * 130:t * 130 + 64], pbb[:, 0:64])
                    sc.copy(vg[:, t * 130 + 65:t * 130 + 129], pbb[:, 64:128])
                    sc.drain()
                    b.seminc(sc, "a_vge", 1)

        for hh in range(2):
            h = 2 * g + hh
            qrow = hh * 64
            with nc.Block(f"a_h{h}") as blk:
                @blk.tensor
                def _(te):
                    if wait_gate and h == 0:
                        te.wait_ge(b.get_sem("g_ct"), NT)
                    for phase in range(2):
                        srcT = qT if phase == 0 else kT
                        dT = distTr if phase == 0 else distT
                        for t in range(NT):
                            qi = (h * 2 + phase) * NT + t
                            if qi >= 1:
                                te.wait_ge(b.get_sem("a_qde"), 2 * qi)
                            base = 1920 - t * 128
                            for j in range(5):
                                w = 512 if j < 4 else 127
                                dst = (PA[:, j * 512:j * 512 + w] if j < 4
                                       else PB[:, 0:127])
                                te.matmul(dst,
                                          srcT[qrow:qrow + 64, t * 128:(t + 1) * 128],
                                          dT[qrow:qrow + 64,
                                             base + j * 512: base + j * 512 + w],
                                          start=True, stop=True)
                            te.drain()
                            b.seminc(te, "a_qdm", 1)

                @blk.scalar
                def _(sc):
                    for phase in range(2):
                        for t in range(NT):
                            qi = (h * 2 + phase) * NT + t
                            sc.wait_ge(b.get_sem("a_qdm"), qi + 1)
                            sc.copy(qd_sb[:, 0:2048], PA[:, :])
                            sc.copy(qd_sb[:, 2048:2175], PB[:, 0:127])
                            sc.drain()
                            b.seminc(sc, "a_qde", 2)

                @blk.sync
                def _(sy):
                    for phase in range(2):
                        dd = qdd if phase == 0 else kdd
                        for t in range(NT):
                            qi = (h * 2 + phase) * NT + t
                            sy.wait_ge(b.get_sem("a_qde"), 2 * (qi + 1))
                            dst = AP(dd, t * 128 * 4095 + (1920 - t * 128),
                                     [[4095, 128], [1, 2175]])
                            b.dma(sy, dst, qd_sb[:, 0:2175], "a_qdw")

            with nc.Block(f"a_s{h}") as blk:
                @blk.sync
                def _(sy):
                    sy.wait_ge(b.get_sem("a_qdw"), 16 * 2 * NT * (h + 1))
                    for lq in range(4):
                        if lq >= 1:
                            sy.wait_ge(b.get_sem("a_pvm"), (h * 4 + lq) * NT)
                        elif h >= 1:
                            sy.wait_ge(b.get_sem("a_pvm"), h * 4 * NT)
                        for ls in range(4):
                            lt = lq * 4 + ls
                            src = AP(qdd, lt * 128 * 4094 + 2047,
                                     [[4094, 128], [1, 2048]])
                            b.dma(sy, relq[:, ls * 2048:(ls + 1) * 2048], src, "a_rq")
                        for rt in range(NT):
                            ri = (h * 4 + lq) * NT + rt
                            if ri >= 2:
                                sy.wait_ge(b.get_sem("a_rkc"), ri - 1)
                            src = AP(kdd, rt * 128 * 4094 + 2047 + lq * 512,
                                     [[4094, 128], [1, 512]])
                            b.dma(sy, relk[:, (ri % 2) * 512:(ri % 2) * 512 + 512],
                                  src, "a_rk")

                @blk.tensor
                def _(te):
                    for lq in range(4):
                        te.wait_ge(b.get_sem("a_rq"), 64 * (h * 4 + lq + 1))
                        for rt in range(NT):
                            ri = (h * 4 + lq) * NT + rt
                            sb = PA[:, (rt % 2) * 512:(rt % 2) * 512 + 512]
                            if ri >= 2:
                                te.wait_ge(b.get_sem("a_exp"), ri - 1)
                            te.matmul(sb, kT[qrow:qrow + 64, rt * 128:(rt + 1) * 128],
                                      qT[qrow:qrow + 64, lq * 512:(lq + 1) * 512],
                                      start=True, stop=False)
                            for ls in range(4):
                                te.matmul(sb[:, ls * 128:(ls + 1) * 128],
                                          relq[:, ls * 2048 + rt * 128:
                                               ls * 2048 + (rt + 1) * 128],
                                          identb[:, :],
                                          start=False, stop=(ls == 3))
                            te.drain()
                            b.seminc(te, "a_scm", 1)
                            if rt >= 1:
                                pri = (h * 4 + lq) * NT + rt - 1
                                te.wait_ge(b.get_sem("a_exp"), pri + 1)
                                if rt - 1 == 0:
                                    te.wait_ge(b.get_sem("a_cxe"), h * 4 + lq)
                                te.matmul(PA[0:65, 1024:1536],
                                          vg[:, (rt - 1) * 130 + hh * 65:
                                             (rt - 1) * 130 + hh * 65 + 65],
                                          expS[:, ((rt - 1) % 2) * 512:
                                               ((rt - 1) % 2) * 512 + 512],
                                          start=(rt - 1 == 0), stop=False)
                                te.drain()
                                b.seminc(te, "a_pvm", 1)
                        ri = (h * 4 + lq) * NT + NT - 1
                        te.wait_ge(b.get_sem("a_exp"), ri + 1)
                        te.matmul(PA[0:65, 1024:1536],
                                  vg[:, (NT - 1) * 130 + hh * 65:
                                     (NT - 1) * 130 + hh * 65 + 65],
                                  expS[:, ((NT - 1) % 2) * 512:
                                       ((NT - 1) % 2) * 512 + 512],
                                  start=False, stop=True)
                        te.drain()
                        b.seminc(te, "a_pvm", 1)
                        te.wait_ge(b.get_sem("a_rr"), h * 4 + lq + 1)
                        te.matmul(PB[0:64, 0:512], onescol2[0:1, 0:64],
                                  rrow[0:1, :], start=True, stop=True)
                        te.drain()
                        b.seminc(te, "a_rbm", 1)

                @blk.vector
                def _(v):
                    for lq in range(4):
                        for rt in range(NT):
                            ri = (h * 4 + lq) * NT + rt
                            sb = PA[:, (rt % 2) * 512:(rt % 2) * 512 + 512]
                            v.wait_ge(b.get_sem("a_scm"), ri + 1)
                            v.wait_ge(b.get_sem("a_rk"), 16 * (ri + 1))
                            v.tensor_tensor(sb, sb,
                                            relk[:, (ri % 2) * 512:(ri % 2) * 512 + 512],
                                            ALU.add)
                            v.drain()
                            b.seminc(v, "a_rkc", 1)
                        v.wait_ge(b.get_sem("a_pvm"), (h * 4 + lq + 1) * NT)
                        v.reciprocal(rrow[0:1, :], PA[64:65, 1024:1536])
                        v.drain()
                        b.seminc(v, "a_rr", 1)
                        v.wait_ge(b.get_sem("a_rbe"), h * 4 + lq + 1)
                        dst = ctx[hh * 64:hh * 64 + 64,
                                  (h // 2) * S + lq * 512:(h // 2) * S + (lq + 1) * 512]
                        v.tensor_tensor(dst, PA[0:64, 1024:1536],
                                        rbc[0:64, :], ALU.mult)
                        v.drain()
                        b.seminc(v, "a_cxe", 1)

                @blk.scalar
                def _(sc):
                    for lq in range(4):
                        for rt in range(NT):
                            ri = (h * 4 + lq) * NT + rt
                            sb = PA[:, (rt % 2) * 512:(rt % 2) * 512 + 512]
                            sc.wait_ge(b.get_sem("a_rkc"), ri + 1)
                            if ri >= 2:
                                sc.wait_ge(b.get_sem("a_pvm"), ri - 1)
                            sc.activation(expS[:, (ri % 2) * 512:(ri % 2) * 512 + 512],
                                          sb, AF.Exp, bias=cshift[:, 0:1], scale=0.125)
                            sc.drain()
                            b.seminc(sc, "a_exp", 1)
                        sc.wait_ge(b.get_sem("a_rbm"), h * 4 + lq + 1)
                        sc.copy(rbc[0:64, :], PB[0:64, 0:512])
                        sc.drain()
                        b.seminc(sc, "a_rbe", 1)

    with nc.Block("a_wo") as blk:
        @blk.gpsimd
        def _(g0):
            b.wait(g0, "a_cxe")
            for j in range(3):
                b.dma(g0, gbc[0:1, j * 1024:(j + 1) * 1024],
                      b.dram["gbo_rows"].ap()[j:j + 1, :], "a_dc2")

        @blk.tensor
        def _(te):
            b.wait(te, "a_cxe")
            b.wait(te, "c_idr")
            te.wait_ge(b.get_sem("a_dc2"), 48)
            for j in range(3):
                if j >= 1:
                    te.wait_ge(b.get_sem("a_gbe"), j)
                for nh in range(2):
                    te.matmul(PB[:, nh * 512:(nh + 1) * 512],
                              onescol2[0:1, :],
                              gbc[0:1, j * 1024 + nh * 512:j * 1024 + (nh + 1) * 512],
                              start=True, stop=True)
                te.drain()
                b.seminc(te, "a_gb", 1)
            te.wait_ge(b.get_sem("a_gbe"), 3)
            for t in range(NT):
                pw = PA[:, 0:1024] if t % 2 == 0 else PA[:, 1024:2048]
                if t >= 2:
                    te.wait_ge(b.get_sem("a_hA"), t - 1)
                for c in range(DC):
                    for nh in range(2):
                        te.matmul(pw[:, nh * 512:(nh + 1) * 512],
                                  ctx[:, c * S + t * 128:c * S + (t + 1) * 128],
                                  wo[:, c * 1024 + nh * 512:c * 1024 + (nh + 1) * 512],
                                  start=(c == 0), stop=(c == DC - 1))
                te.drain()
                b.seminc(te, "a_wom", 1)
                pxn = PB[:, :].bitcast(F32R)
                if t >= 1:
                    te.wait_ge(b.get_sem("a_hA"), t)
                for j in range(DC):
                    te.matmul(pxn[:, j * 128:(j + 1) * 128],
                              xT[:, j * S + t * 128:j * S + (t + 1) * 128],
                              identr[:, :], start=True, stop=True,
                              is_transpose=True)
                te.drain()
                b.seminc(te, "a_xnm", 1)

        @blk.vector
        def _(v):
            v.wait_ge(b.get_sem("a_gbe"), 3)
            for t in range(NT):
                pw = PA[:, 0:1024] if t % 2 == 0 else PA[:, 1024:2048]
                pxn = PB[:, :].bitcast(F32R)
                v.wait_ge(b.get_sem("a_wom"), t + 1)
                v.wait_ge(b.get_sem("a_xnm"), t + 1)
                if t >= 1:
                    v.wait_ge(b.get_sem("a_sq"), t)
                v.tensor_add(tmpA[:, :], pw, gbc[:, 2048:3072])
                v.drain()
                v.tensor_tensor(tmpA[:, :], tmpA[:, :], pxn, ALU.add)
                v.drain()
                b.seminc(v, "a_hA", 1)
                v.reduce_sum(stcol(stats, ST_T2, t), tmpA[:, :], axis=AX)
                v.drain()
                v.wait_ge(b.get_sem("a_sq"), t + 1)
                mu = stcol(stats, ST_T2, t)
                va = stcol(stats, ST_T3, t)
                u0 = stcol(stats, ST_U0, t)
                v.tensor_scalar_mul(mu, mu, 1.0 / D)
                v.tensor_scalar_mul(va, va, 1.0 / D)
                v.drain()
                v.tensor_tensor(u0, mu, mu, ALU.mult)
                v.drain()
                v.tensor_sub(va, va, u0)
                v.drain()
                v.tensor_scalar_add(va, va, float(EPS_ATT))
                v.drain()
                b.seminc(v, "a_var", 1)
                v.wait_ge(b.get_sem("a_rsd"), t + 1)
                if t >= 1:
                    v.wait_ge(b.get_sem("a_out"), 16 * t)
                v.tensor_scalar(tmpB[:, :], tmpA[:, :], mu, u0,
                                ALU.subtract, ALU.mult)
                v.drain()
                v.tensor_tensor(tmpB[:, :], tmpB[:, :], gbc[:, 0:1024], ALU.mult)
                v.drain()
                v.tensor_add(tmpB[:, :], tmpB[:, :], gbc[:, 1024:2048])
                v.drain()
                b.seminc(v, "a_ase", 1)

        @blk.scalar
        def _(sc):
            for j in range(3):
                b.wait(sc, "a_gb", j + 1)
                sc.copy(gbc[:, j * 1024:(j + 1) * 1024], PB[:, :])
                sc.drain()
                b.seminc(sc, "a_gbe", 1)
            for t in range(NT):
                sc.wait_ge(b.get_sem("a_hA"), t + 1)
                sc.activation(sq[:, :], tmpA[:, :], AF.Square,
                              accum_out=stcol(stats, ST_T3, t))
                sc.drain()
                b.seminc(sc, "a_sq", 1)
                sc.wait_ge(b.get_sem("a_var"), t + 1)
                emit_rstd(sc, stcol(stats, ST_U0, t), stcol(stats, ST_T3, t))
                sc.drain()
                b.seminc(sc, "a_rsd", 1)

        @blk.gpsimd
        def _(gg):
            if accumulate:
                gg.wait_ge(b.get_sem("d_x1w"), 16)
            for t in range(NT):
                gg.wait_ge(b.get_sem("a_ase"), t + 1)
                dst = AP(outd, t * 128 * D, [[D, 128], [1, D]])
                if accumulate:
                    b.dma(gg, dst, tmpB[:, :], "a_out", accum_op=ALU.add)
                else:
                    b.dma(gg, dst, tmpB[:, :], "a_out")
            gg.wait_ge(b.get_sem("a_out"), b.cnt["a_out"])




NEWTON_ITERS = 46
LCH = 512
NCH = S // LCH


def r3(apnd):
    return apnd.rearrange("p (k o) -> p k o", o=1)


def emit_l1a_proj(b):
    nc = b.nc
    xT = b.T["xT"]
    PB, PC = b.T["PB"], b.T["PC"]
    wa, wx = b.T["wa"], b.T["wx"]
    tmpA = b.T["tmpA"]

    with nc.Block("l1a_proj") as blk:
        @blk.gpsimd
        def _(g):
            for nm, wt in (("w_aT_perm", wa), ("w_xT_perm", wx)):
                src = AP(b.dram[nm], 0, [[D, 128], [128 * D, DC], [1, D]])
                b.dma(g, wt[:, :].rearrange("p (c m) -> p c m", c=DC), src, "p_dw")

        @blk.tensor
        def _(te):
            b.wait(te, "z_done")
            for wi, wt in ((0, wa), (1, wx)):
                te.wait_ge(b.get_sem("p_dw"), 16 * (wi + 1))
                for ch in range(NCH):
                    for ib in range(8):
                        pj = (wi * NCH + ch) * 8 + ib
                        ph = (PB if pj % 2 == 0 else PC)[:, 0:512]
                        if pj >= 2:
                            te.wait_ge(b.get_sem("p_ev"), pj - 1)
                        for c in range(DC):
                            te.matmul(ph,
                                      wt[:, c * D + ib * 128:c * D + ib * 128 + 128],
                                      xT[:, c * S + ch * LCH:c * S + (ch + 1) * LCH],
                                      start=(c == 0), stop=(c == DC - 1))
                        te.drain()
                        b.seminc(te, "p_mm", 1)

        @blk.scalar
        def _(sc):
            for pj in range(64):
                ph = (PB if pj % 2 == 0 else PC)[:, 0:512]
                sc.wait_ge(b.get_sem("p_mm"), pj + 1)
                if pj >= 2:
                    sc.wait_ge(b.get_sem("p_wr"), 16 * (pj - 1))
                sc.copy(tmpA[:, (pj % 2) * 512:(pj % 2) * 512 + 512], ph)
                sc.drain()
                b.seminc(sc, "p_ev", 1)

        @blk.sync
        def _(s):
            for pj in range(64):
                wi, ch, ib = pj // 32, (pj // 8) % 4, pj % 8
                dd = b.dram["d_a2"] if wi == 0 else b.dram["d_x2"]
                s.wait_ge(b.get_sem("p_ev"), pj + 1)
                dst = AP(dd, ib * S + ch * LCH, [[8 * S, 128], [1, LCH]])
                b.dma(s, dst, tmpA[:, (pj % 2) * 512:(pj % 2) * 512 + 512], "p_wr")


def emit_l1a_inv(b):
    nc = b.nc
    PA, PB, PC = b.T["PA"], b.T["PB"], b.T["PC"]
    ident, identr = b.T["ident"], b.T["identr"]
    Vre, Vim, VimN = b.T["Vre"], b.T["Vim"], b.T["VimN"]
    VTre, VTimN = b.T["VTre"], b.T["VTimN"]
    XTs = [(b.T["XTar"], b.T["XTai"]), (b.T["XTbr"], b.T["XTbi"])]
    Xr, Xi = b.T["Xnr"], b.T["Xni"]
    Rre, Rim, RimN = b.T["Rre"], b.T["Rim"], b.T["RimN"]
    i2b, nrm, sq = b.T["i2b"], b.T["nrm"], b.T["sq"]
    onescl = b.T["onescl"]
    NI = NEWTON_ITERS

    with nc.Block("i_ld") as blk:
        @blk.gpsimd
        def _(g):
            for nm, dst in (("L1_attn_mat_v_real", Vre), ("L1_attn_mat_v_imag", Vim),
                            ("mat_v_T_re", VTre), ("mat_v_T_im", VTimN)):
                src = AP(b.dram[nm], 0, [[512, 128], [128 * 512, 4], [1, 512]])
                b.dma(g, dst[:, :].rearrange("p (c n) -> p c n", c=4), src, "i_dc")
            b.dma(g, b.T["h0c"][:, 0:4],
                  AP(b.dram["L1_attn_last_hidden_real"], 0, [[1, 128], [128, 4]]),
                  "i_dc", allow_slow_non_contiguous=True)
            b.dma(g, b.T["h0c"][:, 4:8],
                  AP(b.dram["L1_attn_last_hidden_imag"], 0, [[1, 128], [128, 4]]),
                  "i_dc", allow_slow_non_contiguous=True)

        @blk.vector
        def _(v):
            v.wait_ge(b.get_sem("i_dc"), b.cnt["i_dc"])
            b.wait(v, "c_id")
            v.memset(onescl[0:1, :], 1.0)
            v.memset(sq[0:1, 12:13], 0.0)
            v.tensor_scalar_mul(VimN[:, :], Vim[:, :], -1.0)
            v.drain()
            v.tensor_scalar_mul(VTimN[:, :], VTimN[:, :], -1.0)
            v.drain()
            for c in range(4):
                v.tensor_reduce(nrm[:, c:c + 1], Vre[:, c * 512:(c + 1) * 512],
                                axis=AX, op=ALU.add, apply_absolute_value=True)
                v.tensor_reduce(nrm[:, 8 + c:9 + c], Vim[:, c * 512:(c + 1) * 512],
                                axis=AX, op=ALU.add, apply_absolute_value=True)
                v.tensor_reduce(nrm[:, 4 + c:5 + c], VTre[:, c * 512:(c + 1) * 512],
                                axis=AX, op=ALU.add, apply_absolute_value=True)
                v.tensor_reduce(nrm[:, 12 + c:13 + c],
                                VTimN[:, c * 512:(c + 1) * 512],
                                axis=AX, op=ALU.add, apply_absolute_value=True)
            v.drain()
            v.tensor_add(nrm[:, 0:8], nrm[:, 0:8], nrm[:, 8:16])
            v.drain()
            b.seminc(v, "i_n", 1)

        @blk.tensor
        def _(te):
            b.wait(te, "i_n")
            te.wait_ge(b.get_sem("p_ev"), 64)
            te.matmul(PB[0:8, 0:128], nrm[:, 0:8], ident[:, :],
                      start=True, stop=True)
            te.drain()
            b.seminc(te, "i_t1", 1)

        @blk.vector
        def _(v):
            b.wait(v, "i_t1")
            v.tensor_copy(sq[0:8, 0:128], PB[0:8, 0:128])
            v.drain()
            v.tensor_reduce(sq[0:8, 128:129], sq[0:8, 0:128], axis=AX, op=ALU.max)
            v.drain()
            b.seminc(v, "i_m1", 1)

        @blk.tensor
        def _(te):
            b.wait(te, "i_m1")
            te.matmul(PB[0:1, 0:8], sq[0:8, 128:129], ident[0:8, 0:8],
                      start=True, stop=True)
            te.drain()
            b.seminc(te, "i_t2", 1)

        @blk.vector
        def _(v):
            b.wait(v, "i_t2")
            v.tensor_copy(sq[0:1, 0:8], PB[0:1, 0:8])
            v.drain()
            v.reduce_max(sq[0:1, 8:9], sq[0:1, 0:4], axis=AX)
            v.reduce_max(sq[0:1, 9:10], sq[0:1, 4:8], axis=AX)
            v.drain()
            v.tensor_tensor(sq[0:1, 10:11], sq[0:1, 8:9], sq[0:1, 9:10], ALU.mult)
            v.drain()
            v.reciprocal(sq[0:1, 11:12], sq[0:1, 10:11])
            v.drain()
            b.seminc(v, "i_c", 1)

        @blk.tensor
        def _(te):
            b.wait(te, "i_c")
            te.matmul(PB[:, 0:2], onescl[0:1, :], sq[0:1, 11:13],
                      start=True, stop=True)
            te.drain()
            b.seminc(te, "i_cb", 1)

        @blk.vector
        def _(v):
            b.wait(v, "i_cb")
            v.tensor_copy(nrm[:, 8:10], PB[:, 0:2])
            v.drain()
            v.wait_ge(b.get_sem("p_mm"), 64)
            v.tensor_scalar(XTs[0][0][:, :], VTre[:, :], nrm[:, 8:9], None,
                            ALU.mult)
            v.drain()
            v.tensor_scalar(XTs[0][1][:, :], VTimN[:, :], nrm[:, 8:9], None,
                            ALU.mult)
            v.drain()
            v.memset(i2b[:, :], 0.0)
            v.drain()
            for rb in range(4):
                v.tensor_scalar_mul(
                    i2b[:, rb * 512 + rb * 128:rb * 512 + rb * 128 + 128],
                    ident[:, :], 2.0)
            v.drain()
            b.seminc(v, "i_x0", 1)

    with nc.Block("i_newton") as blk:
        @blk.tensor
        def _(te):
            b.wait(te, "i_x0")
            b.wait(te, "c_idr")
            for it in range(NI + 1):
                XTr_, XTi_ = XTs[it % 2]
                for pl in range(2):
                    srcT = XTr_ if pl == 0 else XTi_
                    for rb in range(4):
                        ti = it * 8 + pl * 4 + rb
                        px = ((PB if ti % 2 == 0 else PC)[:, :]).bitcast(F32R)
                        if ti >= 2:
                            te.wait_ge(b.get_sem("i_xe"), ti - 1)
                        for cb in range(4):
                            te.matmul(px[:, cb * 128:(cb + 1) * 128],
                                      srcT[:, cb * 512 + rb * 128:
                                           cb * 512 + rb * 128 + 128],
                                      identr[:, :], start=True, stop=True,
                                      is_transpose=True)
                        te.drain()
                        b.seminc(te, "i_xm", 1)
                if it == NI:
                    break
                te.wait_ge(b.get_sem("i_xe"), it * 8 + 8)
                for rb in range(4):
                    wi = it * 4 + rb
                    pw = PA[:, 0:1024] if rb % 2 == 0 else PA[:, 1024:2048]
                    if wi >= 2:
                        te.wait_ge(b.get_sem("i_re"), 3 * (wi - 1))
                    for kb in range(4):
                        te.matmul(pw[:, 0:512],
                                  Vre[:, kb * 512 + rb * 128:
                                      kb * 512 + rb * 128 + 128],
                                  Xr[:, kb * 512:(kb + 1) * 512],
                                  start=(kb == 0), stop=False)
                        te.matmul(pw[:, 0:512],
                                  VimN[:, kb * 512 + rb * 128:
                                       kb * 512 + rb * 128 + 128],
                                  Xi[:, kb * 512:(kb + 1) * 512],
                                  start=False, stop=(kb == 3))
                    for kb in range(4):
                        te.matmul(pw[:, 512:1024],
                                  Vre[:, kb * 512 + rb * 128:
                                      kb * 512 + rb * 128 + 128],
                                  Xi[:, kb * 512:(kb + 1) * 512],
                                  start=(kb == 0), stop=False)
                        te.matmul(pw[:, 512:1024],
                                  Vim[:, kb * 512 + rb * 128:
                                      kb * 512 + rb * 128 + 128],
                                  Xr[:, kb * 512:(kb + 1) * 512],
                                  start=False, stop=(kb == 3))
                    te.drain()
                    b.seminc(te, "i_wm", 1)
                te.wait_ge(b.get_sem("i_re"), 3 * (it * 4 + 4))
                for cb in range(4):
                    ci = it * 4 + cb
                    px2 = PA[:, 0:1024] if cb % 2 == 0 else PA[:, 1024:2048]
                    if ci >= 2:
                        te.wait_ge(b.get_sem("i_ce"), ci - 1)
                    for kb in range(4):
                        te.matmul(px2[:, 0:512],
                                  Rre[:, kb * 512 + cb * 128:
                                      kb * 512 + cb * 128 + 128],
                                  XTr_[:, kb * 512:(kb + 1) * 512],
                                  start=(kb == 0), stop=False)
                        te.matmul(px2[:, 0:512],
                                  RimN[:, kb * 512 + cb * 128:
                                       kb * 512 + cb * 128 + 128],
                                  XTi_[:, kb * 512:(kb + 1) * 512],
                                  start=False, stop=(kb == 3))
                    for kb in range(4):
                        te.matmul(px2[:, 512:1024],
                                  Rre[:, kb * 512 + cb * 128:
                                      kb * 512 + cb * 128 + 128],
                                  XTi_[:, kb * 512:(kb + 1) * 512],
                                  start=(kb == 0), stop=False)
                        te.matmul(px2[:, 512:1024],
                                  Rim[:, kb * 512 + cb * 128:
                                      kb * 512 + cb * 128 + 128],
                                  XTr_[:, kb * 512:(kb + 1) * 512],
                                  start=False, stop=(kb == 3))
                    te.drain()
                    b.seminc(te, "i_cm", 1)
                te.wait_ge(b.get_sem("i_ce"), (it + 1) * 4)

        @blk.scalar
        def _(sc):
            for it in range(NI + 1):
                for t8 in range(8):
                    gi = it * 8 + t8
                    pl, rb = t8 // 4, t8 % 4
                    dstN = Xr if pl == 0 else Xi
                    px = ((PB if gi % 2 == 0 else PC)[:, :]).bitcast(F32R)
                    sc.wait_ge(b.get_sem("i_xm"), gi + 1)
                    sc.copy(dstN[:, rb * 512:(rb + 1) * 512], px[:, 0:512])
                    sc.drain()
                    b.seminc(sc, "i_xe", 1)

        @blk.vector
        def _(v):
            for it in range(NI):
                for rb in range(4):
                    wi = it * 4 + rb
                    pw = PA[:, 0:1024] if rb % 2 == 0 else PA[:, 1024:2048]
                    v.wait_ge(b.get_sem("i_wm"), wi + 1)
                    v.scalar_tensor_tensor(Rre[:, rb * 512:(rb + 1) * 512],
                                           pw[:, 0:512], -1.0,
                                           i2b[:, rb * 512:(rb + 1) * 512],
                                           ALU.mult, ALU.add)
                    v.drain()
                    v.tensor_scalar_mul(Rim[:, rb * 512:(rb + 1) * 512],
                                        pw[:, 512:1024], -1.0)
                    v.drain()
                    v.tensor_copy(RimN[:, rb * 512:(rb + 1) * 512],
                                  pw[:, 512:1024])
                    v.drain()
                    b.seminc(v, "i_re", 3)
                XTnr_, XTni_ = XTs[(it + 1) % 2]
                for cb in range(4):
                    ci = it * 4 + cb
                    px2 = PA[:, 0:1024] if cb % 2 == 0 else PA[:, 1024:2048]
                    v.wait_ge(b.get_sem("i_cm"), ci + 1)
                    v.tensor_copy(XTnr_[:, cb * 512:(cb + 1) * 512], px2[:, 0:512])
                    v.drain()
                    v.tensor_copy(XTni_[:, cb * 512:(cb + 1) * 512],
                                  px2[:, 512:1024])
                    v.drain()
                    b.seminc(v, "i_ce", 1)


def emit_l1a_scan(b, dest, accumulate):
    nc = b.nc
    PA, PB, PC = b.T["PA"], b.T["PB"], b.T["PC"]
    VTre, VTimN = b.T["VTre"], b.T["VTimN"]
    Xr, Xi, XniN = b.T["Xnr"], b.T["Xni"], b.T["XniN"]
    a2, x2, uv = b.T["a2"], b.T["x2"], b.T["uv"]
    af, arr, sconv = b.T["af"], b.T["arr"], b.T["sconv"]
    vsc, hre = b.T["vsc"], b.T["hre"]
    woutT, ostg = b.T["woutT"], b.T["ostg"]
    carry, h0c = b.T["carry"], b.T["h0c"]
    outd = b.dram[dest]
    NI = NEWTON_ITERS

    with nc.Block("l1a_pre") as blk:
        @blk.gpsimd
        def _(g):
            g.wait_ge(b.get_sem("i_wm"), NI * 4)
            src = AP(b.dram["w_outT"], 0, [[D, 128], [128 * D, 4], [1, D]])
            b.dma(g, woutT[:, :].rearrange("p (c n) -> p c n", c=4), src, "l_dwo")

        @blk.vector
        def _(v):
            v.wait_ge(b.get_sem("i_xe"), (NI + 1) * 8)
            v.wait_ge(b.get_sem("i_wm"), NI * 4)
            v.tensor_scalar_mul(XniN[:, :], Xi[:, :], -1.0)
            v.drain()
            v.tensor_copy(carry[:, 0:8], h0c[:, 0:8])
            v.memset(carry[:, 8:16], 0.0)
            v.drain()
            b.seminc(v, "l_pre", 1)

    for ch in range(NCH):
        c0 = ch * LCH
        with nc.Block(f"l1a_c{ch}") as blk:
            @blk.sync
            def _(s):
                if ch == 0:
                    s.wait_ge(b.get_sem("p_wr"), 16 * 64)
                    s.wait_ge(b.get_sem("i_xm"), (NI + 1) * 8)
                else:
                    s.wait_ge(b.get_sem("l_hev"), ch * 4)
                for dd, dst in ((b.dram["d_a2"], a2), (b.dram["d_x2"], x2)):
                    src = AP(dd, c0, [[8 * S, 128], [S, 8], [1, LCH]])
                    b.dma(s, dst[:, :].rearrange("p (k t) -> p k t", k=8), src,
                          "l_dc")

            @blk.vector
            def _(v):
                v.wait_ge(b.get_sem("l_dc"), 32 * (ch + 1))
                if ch == 0:
                    v.wait_ge(b.get_sem("l_pre"), 1)
                v.tensor_tensor(af[:, 0:2048], a2[:, 0:2048], a2[:, 0:2048],
                                ALU.mult)
                v.tensor_tensor(af[:, 2048:4096], a2[:, 2048:4096],
                                a2[:, 2048:4096], ALU.mult)
                v.drain()
                v.tensor_add(af[:, 0:2048], af[:, 0:2048], af[:, 2048:4096])
                v.drain()
                b.seminc(v, "l_m2", 1)
                v.wait_ge(b.get_sem("l_sq"), 2 * ch + 1)
                v.tensor_scalar_add(af[:, 0:2048], af[:, 0:2048], 1.0)
                v.drain()
                v.reciprocal(af[:, 0:2048], af[:, 0:2048])
                v.drain()
                v.tensor_tensor(af[:, 0:2048], af[:, 0:2048], af[:, 2048:4096],
                                ALU.mult)
                v.drain()
                v.tensor_tensor(a2[:, 0:2048], a2[:, 0:2048], af[:, 0:2048],
                                ALU.mult)
                v.tensor_tensor(a2[:, 2048:4096], a2[:, 2048:4096],
                                af[:, 0:2048], ALU.mult)
                v.drain()
                v.tensor_tensor(af[:, 0:2048], x2[:, 0:2048], x2[:, 0:2048],
                                ALU.mult)
                v.tensor_tensor(af[:, 2048:4096], x2[:, 2048:4096],
                                x2[:, 2048:4096], ALU.mult)
                v.drain()
                v.tensor_add(af[:, 0:2048], af[:, 0:2048], af[:, 2048:4096])
                v.drain()
                b.seminc(v, "l_m2", 1)
                v.wait_ge(b.get_sem("l_sq"), 2 * ch + 2)
                v.tensor_tensor(x2[:, 0:2048], x2[:, 0:2048], af[:, 2048:4096],
                                ALU.mult)
                v.tensor_tensor(x2[:, 2048:4096], x2[:, 2048:4096],
                                af[:, 2048:4096], ALU.mult)
                v.drain()
                av = arr[:, :].rearrange("p (k t) -> p k t", k=8)
                xv = x2[:, :].rearrange("p (k t) -> p k t", k=8)
                v.tensor_copy(av[:, :, 1:LCH], xv[:, :, 0:LCH - 1])
                v.drain()
                v.tensor_copy(av[:, :, 0:1], r3(carry[:, 0:8]))
                v.drain()
                v.tensor_copy(r3(carry[:, 0:8]), xv[:, :, LCH - 1:LCH])
                v.drain()
                b.seminc(v, "l_xr", 1)
                for mb in range(4):
                    pu = PA[:, 0:1024] if mb % 2 == 0 else PA[:, 1024:2048]
                    v.wait_ge(b.get_sem("l_umm"), ch * 4 + mb + 1)
                    v.tensor_tensor(vsc[:, 0:512], pu[:, 0:512],
                                    a2[:, mb * 512:(mb + 1) * 512], ALU.mult)
                    v.tensor_tensor(vsc[:, 512:1024], pu[:, 512:1024],
                                    a2[:, 2048 + mb * 512:2048 + (mb + 1) * 512],
                                    ALU.mult)
                    v.drain()
                    v.tensor_sub(uv[:, mb * 512:(mb + 1) * 512],
                                 vsc[:, 0:512], vsc[:, 512:1024])
                    v.drain()
                    v.tensor_tensor(vsc[:, 0:512], pu[:, 512:1024],
                                    a2[:, mb * 512:(mb + 1) * 512], ALU.mult)
                    v.tensor_tensor(vsc[:, 512:1024], pu[:, 0:512],
                                    a2[:, 2048 + mb * 512:2048 + (mb + 1) * 512],
                                    ALU.mult)
                    v.drain()
                    v.tensor_add(uv[:, 2048 + mb * 512:2048 + (mb + 1) * 512],
                                 vsc[:, 0:512], vsc[:, 512:1024])
                    v.drain()
                    b.seminc(v, "l_vev", 1)
                a2v = a2[:, :].rearrange("p (k t) -> p k t", k=8)
                uvv = uv[:, :].rearrange("p (k t) -> p k t", k=8)
                v.tensor_copy(r3(vsc[:, 0:8]), a2v[:, :, 0:1])
                v.drain()
                v.tensor_tensor(vsc[:, 8:12], vsc[:, 0:4], carry[:, 8:12],
                                ALU.mult)
                v.tensor_tensor(vsc[:, 12:16], vsc[:, 4:8], carry[:, 12:16],
                                ALU.mult)
                v.tensor_tensor(vsc[:, 16:20], vsc[:, 0:4], carry[:, 12:16],
                                ALU.mult)
                v.tensor_tensor(vsc[:, 20:24], vsc[:, 4:8], carry[:, 8:12],
                                ALU.mult)
                v.drain()
                v.tensor_sub(vsc[:, 8:12], vsc[:, 8:12], vsc[:, 12:16])
                v.tensor_add(vsc[:, 16:20], vsc[:, 16:20], vsc[:, 20:24])
                v.drain()
                v.tensor_add(uvv[:, 0:4, 0:1], uvv[:, 0:4, 0:1],
                             r3(vsc[:, 8:12]))
                v.tensor_add(uvv[:, 4:8, 0:1], uvv[:, 4:8, 0:1],
                             r3(vsc[:, 16:20]))
                v.drain()
                Are, Aim = a2[:, 0:2048], a2[:, 2048:4096]
                Bre, Bim = uv[:, 0:2048], uv[:, 2048:4096]

                def sl(buf, d2, idx, k0, kn):
                    return buf.rearrange("p (k s) -> p k s",
                                         s=d2)[:, k0:k0 + kn, idx:idx + 1]

                for lev in range(9):
                    d = 1 << lev
                    d2 = 2 * d
                    kn = 2048 // d2
                    At_r = sl(Are, d2, d2 - 1, 0, kn)
                    At_i = sl(Aim, d2, d2 - 1, 0, kn)
                    Aj_r = sl(Are, d2, d - 1, 0, kn)
                    Aj_i = sl(Aim, d2, d - 1, 0, kn)
                    Bt_r = sl(Bre, d2, d2 - 1, 0, kn)
                    Bt_i = sl(Bim, d2, d2 - 1, 0, kn)
                    Bj_r = sl(Bre, d2, d - 1, 0, kn)
                    Bj_i = sl(Bim, d2, d - 1, 0, kn)
                    T1 = r3(af[:, 0:kn])
                    T2 = r3(af[:, 1024:1024 + kn])
                    T3 = r3(af[:, 2048:2048 + kn])
                    T4 = r3(af[:, 3072:3072 + kn])
                    v.tensor_tensor(T1, At_r, Bj_r, ALU.mult)
                    v.tensor_tensor(T2, At_i, Bj_i, ALU.mult)
                    v.tensor_tensor(T3, At_r, Bj_i, ALU.mult)
                    v.tensor_tensor(T4, At_i, Bj_r, ALU.mult)
                    v.drain()
                    v.tensor_sub(T1, T1, T2)
                    v.tensor_add(T3, T3, T4)
                    v.drain()
                    v.tensor_add(Bt_r, Bt_r, T1)
                    v.tensor_add(Bt_i, Bt_i, T3)
                    v.drain()
                    v.tensor_tensor(T1, At_r, Aj_r, ALU.mult)
                    v.tensor_tensor(T2, At_i, Aj_i, ALU.mult)
                    v.tensor_tensor(T3, At_r, Aj_i, ALU.mult)
                    v.tensor_tensor(T4, At_i, Aj_r, ALU.mult)
                    v.drain()
                    v.tensor_sub(At_r, T1, T2)
                    v.tensor_add(At_i, T3, T4)
                    v.drain()
                for lev in range(7, -1, -1):
                    d = 1 << lev
                    d2 = 2 * d
                    cnt = 512 // d2 - 1
                    for c in range(4):
                        def pl(buf):
                            return buf[:, c * 512:(c + 1) * 512].rearrange(
                                "p (k s) -> p k s", s=d2)
                        At_r = pl(Are)[:, 1:1 + cnt, d - 1:d]
                        At_i = pl(Aim)[:, 1:1 + cnt, d - 1:d]
                        Bt_r = pl(Bre)[:, 1:1 + cnt, d - 1:d]
                        Bt_i = pl(Bim)[:, 1:1 + cnt, d - 1:d]
                        Bj_r = pl(Bre)[:, 0:cnt, d2 - 1:d2]
                        Bj_i = pl(Bim)[:, 0:cnt, d2 - 1:d2]
                        T1 = r3(af[:, 0:cnt])
                        T2 = r3(af[:, 1024:1024 + cnt])
                        T3 = r3(af[:, 2048:2048 + cnt])
                        T4 = r3(af[:, 3072:3072 + cnt])
                        v.tensor_tensor(T1, At_r, Bj_r, ALU.mult)
                        v.tensor_tensor(T2, At_i, Bj_i, ALU.mult)
                        v.tensor_tensor(T3, At_r, Bj_i, ALU.mult)
                        v.tensor_tensor(T4, At_i, Bj_r, ALU.mult)
                        v.drain()
                        v.tensor_sub(T1, T1, T2)
                        v.tensor_add(T3, T3, T4)
                        v.drain()
                        v.tensor_add(Bt_r, Bt_r, T1)
                        v.tensor_add(Bt_i, Bt_i, T3)
                        v.drain()
                v.tensor_copy(r3(carry[:, 8:16]), uvv[:, :, LCH - 1:LCH])
                v.drain()
                v.tensor_copy(sconv[:, :], uv[:, :])
                v.drain()
                b.seminc(v, "l_scan", 1)
                for jb in range(4):
                    ph = (PB if jb % 2 == 0 else PC)[:, 0:512]
                    v.wait_ge(b.get_sem("l_hmm"), ch * 4 + jb + 1)
                    v.tensor_tensor(hre[:, jb * 512:(jb + 1) * 512], ph,
                                    x2[:, jb * 512:(jb + 1) * 512], ALU.add)
                    v.drain()
                    b.seminc(v, "l_hev", 1)

            @blk.tensor
            def _(te):
                if ch == 0:
                    te.wait_ge(b.get_sem("i_xe"), (NI + 1) * 8)
                    te.wait_ge(b.get_sem("l_dwo"), 16)
                    te.wait_ge(b.get_sem("l_pre"), 1)
                else:
                    te.wait_ge(b.get_sem("l_oev"), ch * 4)
                te.wait_ge(b.get_sem("l_xr"), ch + 1)
                for mb in range(4):
                    pu = PA[:, 0:1024] if mb % 2 == 0 else PA[:, 1024:2048]
                    if ch * 4 + mb >= 2:
                        te.wait_ge(b.get_sem("l_vev"), ch * 4 + mb - 1)
                    for nb in range(4):
                        te.matmul(pu[:, 0:512],
                                  Xr[:, nb * 512 + mb * 128:
                                     nb * 512 + mb * 128 + 128],
                                  arr[:, nb * 512:(nb + 1) * 512],
                                  start=(nb == 0), stop=False)
                        te.matmul(pu[:, 0:512],
                                  XniN[:, nb * 512 + mb * 128:
                                       nb * 512 + mb * 128 + 128],
                                  arr[:, 2048 + nb * 512:2048 + (nb + 1) * 512],
                                  start=False, stop=(nb == 3))
                    for nb in range(4):
                        te.matmul(pu[:, 512:1024],
                                  Xr[:, nb * 512 + mb * 128:
                                     nb * 512 + mb * 128 + 128],
                                  arr[:, 2048 + nb * 512:2048 + (nb + 1) * 512],
                                  start=(nb == 0), stop=False)
                        te.matmul(pu[:, 512:1024],
                                  Xi[:, nb * 512 + mb * 128:
                                     nb * 512 + mb * 128 + 128],
                                  arr[:, nb * 512:(nb + 1) * 512],
                                  start=False, stop=(nb == 3))
                    te.drain()
                    b.seminc(te, "l_umm", 1)
                te.wait_ge(b.get_sem("l_scan"), ch + 1)
                for jb in range(4):
                    ph = (PB if jb % 2 == 0 else PC)[:, 0:512]
                    if ch * 4 + jb >= 2:
                        te.wait_ge(b.get_sem("l_hev"), ch * 4 + jb - 1)
                    for mb in range(4):
                        te.matmul(ph,
                                  VTre[:, mb * 512 + jb * 128:
                                       mb * 512 + jb * 128 + 128],
                                  sconv[:, mb * 512:(mb + 1) * 512],
                                  start=(mb == 0), stop=False)
                        te.matmul(ph,
                                  VTimN[:, mb * 512 + jb * 128:
                                        mb * 512 + jb * 128 + 128],
                                  sconv[:, 2048 + mb * 512:2048 + (mb + 1) * 512],
                                  start=False, stop=(mb == 3))
                    te.drain()
                    b.seminc(te, "l_hmm", 1)
                te.wait_ge(b.get_sem("l_hev"), ch * 4 + 4)
                for q in range(4):
                    po = PA[:, 0:1024] if q % 2 == 0 else PA[:, 1024:2048]
                    oi = ch * 4 + q
                    if oi >= 2:
                        te.wait_ge(b.get_sem("l_oev"), oi - 1)
                    for jb in range(4):
                        for nh in range(2):
                            te.matmul(po[:, nh * 512:(nh + 1) * 512],
                                      hre[:, jb * 512 + q * 128:
                                          jb * 512 + q * 128 + 128],
                                      woutT[:, jb * 1024 + nh * 512:
                                            jb * 1024 + (nh + 1) * 512],
                                      start=(jb == 0), stop=(jb == 3))
                    te.drain()
                    b.seminc(te, "l_omm", 1)

            @blk.scalar
            def _(sc):
                for half in range(2):
                    sc.wait_ge(b.get_sem("l_m2"), 2 * ch + half + 1)
                    sc.activation(af[:, 2048:4096], af[:, 0:2048], AF.Ln)
                    sc.drain()
                    sc.activation(af[:, 2048:4096], af[:, 2048:4096], AF.Exp,
                                  scale=0.5)
                    sc.drain()
                    b.seminc(sc, "l_sq", 1)
                for q in range(4):
                    po = PA[:, 0:1024] if q % 2 == 0 else PA[:, 1024:2048]
                    oi = ch * 4 + q
                    sc.wait_ge(b.get_sem("l_omm"), oi + 1)
                    if oi >= 2:
                        sc.wait_ge(b.get_sem("l_out"), 16 * (oi - 1))
                    sc.copy(ostg[:, (q % 2) * 1024:(q % 2) * 1024 + 1024], po)
                    sc.drain()
                    b.seminc(sc, "l_oev", 1)

            @blk.gpsimd
            def _(g):
                for q in range(4):
                    oi = ch * 4 + q
                    g.wait_ge(b.get_sem("l_oev"), oi + 1)
                    dst = AP(outd, (c0 + q * 128) * D, [[D, 128], [1, D]])
                    if accumulate:
                        b.dma(g, dst, ostg[:, (q % 2) * 1024:(q % 2) * 1024 + 1024],
                              "l_out", accum_op=ALU.add)
                    else:
                        b.dma(g, dst, ostg[:, (q % 2) * 1024:(q % 2) * 1024 + 1024],
                              "l_out")
                if ch == NCH - 1:
                    g.wait_ge(b.get_sem("l_out"), 16 * 4 * NCH)


def emit_l1a(b, dest, accumulate):
    emit_l1a_proj(b)
    emit_l1a_inv(b)
    emit_l1a_scan(b, dest, accumulate)


def emit_idsetup(b):
    nc = b.nc
    with nc.Block("idsetup") as blk:
        @blk.gpsimd
        def _(g):
            g.iota(b.T["iota_t"][:, :], pattern=[[1, 128]], base=0,
                   channel_multiplier=-1,
                   allow_small_or_imprecise_dtypes=True)
            g.drain()
            b.seminc(g, "c_id", 1)

        @blk.vector
        def _(v):
            b.wait(v, "c_id")
            v.tensor_scalar(b.T["ident"][:, :], b.T["iota_t"][:, :], 0.0, None,
                            ALU.is_equal)
            v.drain()
            v.tensor_copy(b.T["identr"][:, :], b.T["ident"][:, :])
            v.drain()
            b.seminc(v, "c_id", 1)
            b.seminc(v, "c_idr", 1)


def alloc_l1a(b):
    b.din("L1_attn_mat_v_real", [INNER, INNER], F32)
    b.din("L1_attn_mat_v_imag", [INNER, INNER], F32)
    b.din("mat_v_T_re", [INNER, INNER], F32)
    b.din("mat_v_T_im", [INNER, INNER], F32)
    b.din("w_aT_perm", [D, D], F32)
    b.din("w_xT_perm", [D, D], F32)
    b.din("w_outT", [INNER, D], F32)
    b.din("L1_attn_last_hidden_real", [INNER], F32)
    b.din("L1_attn_last_hidden_imag", [INNER], F32)
    b.dint("d_a2", [128, 8 * S], F32)
    b.dint("d_x2", [128, 8 * S], F32)
    for nm in ("Vre", "Vim", "VimN", "VTre", "VTimN", "Xnr", "Xni"):
        b.sb(nm, [128, 4 * 512], F32R)
    b.sb("vsc", [128, 1024], F32)
    b.sb("hre", [128, 4 * 512], BF16)
    b.sb("nrm", [128, 16], F32)
    b.sb("onescl", [1, 128], F32)
    b.sb("h0c", [128, 8], F32)
    b.sb("carry", [128, 16], F32)
    b.sb("XniN", [128, 4 * 512], F32R, at=b.off["Vre"])
    b.sb("woutT", [128, 4 * 1024], BF16, at=b.off["Vim"])
    b.sb("ostg", [128, 2 * 1024], F32, at=b.off["VimN"])
    xs = b.off["xstage"]
    b.sb("wa", [128, DC * D], F32R, at=xs)
    b.sb("wx", [128, DC * D], F32R, at=xs + 32768)
    b.sb("XTar", [128, 4 * 512], F32R, at=xs)
    b.sb("XTai", [128, 4 * 512], F32R, at=xs + 8192)
    b.sb("XTbr", [128, 4 * 512], F32R, at=xs + 16384)
    b.sb("XTbi", [128, 4 * 512], F32R, at=xs + 24576)
    b.sb("Rre", [128, 4 * 512], F32R, at=xs + 32768)
    b.sb("Rim", [128, 4 * 512], F32R, at=xs + 40960)
    b.sb("RimN", [128, 4 * 512], F32R, at=xs + 49152)
    b.sb("i2b", [128, 4 * 512], F32, at=xs + 57344)
    b.sb("a2", [128, 8 * 512], F32, at=xs)
    b.sb("x2", [128, 8 * 512], F32, at=xs + 16384)
    b.sb("uv", [128, 8 * 512], F32, at=xs + 32768)
    b.sb("af", [128, 4096], F32, at=xs + 49152)
    b.sb("arr", [128, 4096], F32R, at=xs + 49152)
    b.sb("sconv", [128, 4096], F32R, at=xs)


def alloc_common(b):
    b.sb("stats", [128, 14 * NT], F32)
    b.sb("xT", [128, DC * S], F32R)
    b.sb("xstage", [128, NT * D], F32)
    b.sb("tmpA", [128, D], F32)
    b.sb("tmpB", [128, D], F32)
    b.sb("sq", [128, D], F32)
    b.sb("ident", [128, 128], F32)
    b.sb("identr", [128, 128], F32R)
    b.sb("iota_t", [128, 128], F32)
    b.ps("PA", [128, 2048])
    b.ps("PB", [128, 1024])
    b.ps("PC", [128, 1024])


def alloc_moe(b):
    b.din("wg_cols", [128, DC * 8], F32)
    b.din("b1_cols", [128, E_MOE * 32], F32)
    b.din("gam_cols", [128, E_MOE * DC], F32)
    b.din("bet_cols", [128, E_MOE * DC], F32)
    b.din("L0_ffn_b2", [E_MOE, D], F32)
    b.din("L0_ffn_W1", [E_MOE * D * F_MOE], F32)
    b.din("L0_ffn_W2", [E_MOE * F_MOE * D], F32)
    b.sb("wg", [128, DC * 8], F32R)
    b.sb("scg", [128, 8], F32)
    b.sb("m8", [128, 8], F32)
    b.sb("gcol", [128, 8], F32)
    b.sb("geq", [128, 16], F32)
    b.sb("cw", [128, NT * 8], F32)
    b.sb("cwT", [8, S], F32R)
    b.sb("y_acc", [128, 8 * 1024], F32)
    b.sb("w1A", [128, DC * 256], F32R)
    b.sb("w1B", [128, DC * 256], F32R)
    b.sb("w2A", [128, 512], BF16)
    b.sb("w2B", [128, 512], BF16)
    b.sb("b1c", [128, E_MOE * 32], F32)
    b.sb("gamc", [128, E_MOE * DC], F32)
    b.sb("betc", [128, E_MOE * DC + 2], F32R)
    b.sb("hb", [128, E_MOE * 32], F32)
    b.sb("b2s", [8, D], F32R)


def alloc_l1f(b):
    b.din("lng_cols", [128, DC], F32)
    b.din("lnb_cols", [128, DC], F32)
    b.din("bn_cols", [128, 4 * DC], F32)
    b.din("wr_cols", [128, DC * 8], F32)
    b.din("br_row", [1, 8], F32)
    b.din("wdn_cols", [128, E_MOE * DC * 64], F32)
    b.din("wup_rows", [64, E_MOE * 1024], F32)
    b.din("b_downT", [64, 8], F32)
    b.din("L1_ffn_b_up", [8, 1024], F32)
    b.dint("ohdram", [16], F32)
    b.sb("lng", [128, DC], F32)
    b.sb("lnb", [128, DC], F32)
    b.sb("zsum", [128, DC], F32)
    b.sb("bnst", [128, 4 * DC], F32)
    b.sb("wr", [128, DC * 8], F32)
    b.sb("brrow", [1, 8], F32)
    b.sb("prow", [1, 16], F32)
    b.sb("bcast", [128, 16], F32)
    b.sb("onescol", [1, 128], F32)
    b.sb("ohcol", [16, 1], F32)
    b.sb("wdn", [128, E_MOE * DC * 64], BF16)
    b.sb("wup", [64, E_MOE * 1024], BF16)
    b.sb("bdT", [64, 8], F32)
    b.sb("bup", [8, 1024], F32)
    b.sb("wdsel", [128, DC * 64], F32R)
    b.sb("wupsel", [66, 1024], F32R)
    b.sb("bdsel", [64, 8], F32)
    b.sb("hdnT", [66, S], F32R)
    b.sb("onesrow", [1, S], F32)
    b.sb("youts", [128, 2 * D], F32)


def alloc_moe_chain(b):
    b.din("wg_cols", [128, DC * 8], F32)
    b.din("b1_cols", [128, E_MOE * 32], F32)
    b.din("gam_cols", [128, E_MOE * DC], F32)
    b.din("bet_cols", [128, E_MOE * DC], F32)
    b.din("L0_ffn_b2", [E_MOE, D], F32)
    b.din("L0_ffn_W1", [E_MOE * D * F_MOE], F32)
    b.din("L0_ffn_W2", [E_MOE * F_MOE * D], F32)
    b.sb("wg", [128, DC * 8], F32R)
    b.sb("scg", [128, 8], F32)
    b.sb("m8", [128, 8], F32)
    b.sb("gcol", [128, 8], F32)
    b.sb("geq", [128, 16], F32)
    b.sb("cw", [128, NT * 8], F32)
    b.sb("cwT", [8, S], F32R)
    b.sb("gamc", [128, E_MOE * DC], F32)
    b.sb("betc", [128, E_MOE * DC + 2], F32R)
    b.sb("y_acc", [128, 8 * 1024], F32, at=b.off["distT"])
    b.sb("w1A", [128, DC * 256], F32R, at=b.off["kT"])
    b.sb("w1B", [128, DC * 256], F32R, at=b.off["vg"])
    b.sb("w2A", [128, 512], BF16, at=b.off["rrow"])
    b.sb("w2B", [128, 512], BF16, at=b.off["rrow"] + 1024)
    b.sb("hb", [128, E_MOE * 32], F32, at=b.off["rbc"])
    b.sb("b1c", [128, E_MOE * 32], F32, at=b.off["rbc"] + 1024)
    b.sb("b2s", [8, D], F32R, at=b.off["tmpA"])


def alloc_att(b):
    b.din("L0_attn_Wq", [D, D], F32)
    b.din("L0_attn_Wk", [D, D], F32)
    b.din("L0_attn_Wv", [D, D], F32)
    b.din("dist_padT", [128, 4095], F32)
    b.din("dist_padT_rev", [128, 4095], F32)
    b.din("bq_cols", [128, 8], F32)
    b.din("bk_cols", [128, 8], F32)
    b.din("bv_cols", [128, 8], F32)
    b.din("wo_rows", [128, DC * 1024], F32)
    b.din("gbo_rows", [3, 1024], F32)
    b.dram["qdrev_dram"] = b.nc.dram_tensor("qdrev_dram", [S * 4095], BF16, kind="Internal")
    b.dram["kd_dram"] = b.nc.dram_tensor("kd_dram", [S * 4095], BF16, kind="Internal")
    b.sb("identb", [128, 128], BF16)
    b.sb("distT", [128, 4095], BF16)
    b.sb("distTr", [128, 4095], BF16)
    b.sb("wq", [128, DC * 128], F32R)
    b.sb("wk", [128, DC * 128], F32R)
    b.sb("wv", [128, DC * 128], F32R)
    b.sb("qT", [128, S], BF16)
    b.sb("kT", [128, S], BF16)
    b.sb("vT", [128, S], BF16)
    b.sb("vg", [128, NT * 130], BF16)
    b.sb("relk", [128, 2 * 512], BF16)
    b.sb("expS", [128, 2 * 512], BF16)
    b.sb("rrow", [1, 512], F32)
    b.sb("rbc", [64, 512], F32)
    b.sb("bqc", [128, 8], F32)
    b.sb("bkc", [128, 8], F32)
    b.sb("bvc", [128, 8], F32)
    b.sb("cshift", [128, 1], F32)
    b.sb("onescol2", [1, 128], F32)
    xs = b.off["xstage"]
    b.sb("relq", [128, 4 * 2048], BF16, at=xs)
    b.sb("gbc", [128, 3 * 1024], F32, at=xs)
    b.sb("qd_sb", [128, 2176], BF16, at=xs)
    b.sb("ctx", [128, 8 * S], BF16, at=xs + 16 * 1024)
    b.sb("wo", [128, DC * 1024], BF16, at=xs + 48 * 1024)


def emit_sum_build(b):
    nc = b.nc
    b.din("sa", [S, D], F32)
    b.din("sb", [S, D], F32)
    b.din("sc", [S, D], F32)
    b.dout("sum_out", [S, D], F32)
    bufs = b.sb("sumbuf", [128, 8 * D], F32)
    with nc.Block("sum") as blk:
        @blk.sync
        def _(s):
            for t in range(NT):
                sl = (t % 2) * 4 * D
                if t >= 2:
                    s.wait_ge(b.get_sem("s_add"), t - 1)
                for i, nm in enumerate(("sa", "sb", "sc")):
                    src = AP(b.dram[nm], t * 128 * D, [[D, 128], [1, D]])
                    b.dma(s, bufs[:, sl + i * D: sl + (i + 1) * D], src, "s_ld")

        @blk.vector
        def _(v):
            for t in range(NT):
                sl = (t % 2) * 4 * D
                v.wait_ge(b.get_sem("s_ld"), 48 * (t + 1))
                if t >= 2:
                    v.wait_ge(b.get_sem("s_out"), 16 * (t - 1))
                v.tensor_add(bufs[:, sl + 3 * D: sl + 4 * D],
                             bufs[:, sl: sl + D], bufs[:, sl + D: sl + 2 * D])
                v.drain()
                v.tensor_add(bufs[:, sl + 3 * D: sl + 4 * D],
                             bufs[:, sl + 3 * D: sl + 4 * D],
                             bufs[:, sl + 2 * D: sl + 3 * D])
                v.drain()
                b.seminc(v, "s_add", 1)

        @blk.gpsimd
        def _(g):
            for t in range(NT):
                sl = (t % 2) * 4 * D
                g.wait_ge(b.get_sem("s_add"), t + 1)
                dst = AP(b.dram["sum_out"], t * 128 * D, [[D, 128], [1, D]])
                b.dma(g, dst, bufs[:, sl + 3 * D: sl + 4 * D], "s_out")
            g.wait_ge(b.get_sem("s_out"), 16 * NT)


def build_dev(which="p0"):
    b = Bld()
    nc = b.nc
    if which == "sum":
        emit_sum_build(b)
        b.es.close()
        return b
    b.din("x0", [S, D], F32)
    if which == "c0":
        b.dout("x1_out", [S, D], F32)
        b.dram["x1dram"] = b.dram["x1_out"]
    else:
        b.dint("x1dram", [S, D], F32)
    alloc_common(b)

    if which == "p0":
        b.dout("xT_dump", [128, DC * S], F32)
        b.dout("stats_dump", [128, 14 * NT], F32)
        emit_p0(b)
        with nc.Block("dump") as blk:
            @blk.gpsimd
            def _(g):
                b.wait(g, "c_ev")
                b.wait(g, "c_aff")
                b.dma(g, b.dram["xT_dump"].ap(), b.T["xT"][:, :], "d_out")
                b.dma(g, b.dram["stats_dump"].ap(), b.T["stats"][:, :], "d_out")
                g.wait_ge(b.get_sem("d_out"), b.cnt["d_out"])
    elif which == "att":
        alloc_att(b)
        b.dout("a0_out", [S, D], F32)
        b.dout("dbg_qT", [128, S], F32)
        b.dout("dbg_kT", [128, S], F32)
        b.dout("dbg_ctx", [128, 8 * S], F32)
        b.dout("dbg_vg", [128, NT * 130], F32)
        b.dout("dbg_ast", [128, 14 * NT], F32)
        b.dout("dbg_hatt", [128, D], F32)
        b.dout("dbg_gbc", [128, 3 * 1024], F32)
        emit_p0(b)
        emit_l0attn(b, "a0_out", accumulate=False)
        with nc.Block("att_dbg") as blk:
            @blk.gpsimd
            def _(g):
                b.wait(g, "a_out")
                b.dma(g, b.dram["dbg_qT"].ap(), b.T["qT"][:, :], "a_dbg")
                b.dma(g, b.dram["dbg_kT"].ap(), b.T["kT"][:, :], "a_dbg")
                b.dma(g, b.dram["dbg_ctx"].ap(), b.T["ctx"][:, :], "a_dbg")
                b.dma(g, b.dram["dbg_vg"].ap(), b.T["vg"][:, :], "a_dbg")
                b.dma(g, b.dram["dbg_ast"].ap(), b.T["stats"][:, :], "a_dbg")
                b.dma(g, b.dram["dbg_hatt"].ap(), b.T["tmpA"][:, :], "a_dbg")
                b.dma(g, b.dram["dbg_gbc"].ap(), b.T["gbc"][:, :], "a_dbg")
                g.wait_ge(b.get_sem("a_dbg"), b.cnt["a_dbg"])
    elif which == "l1f":
        b.din("x1", [S, D], F32)
        alloc_l1f(b)
        b.dout("f1_out", [S, D], F32)
        with nc.Block("idsetup") as blk:
            @blk.gpsimd
            def _(g):
                b.inc(g.iota(b.T["iota_t"][:, :], pattern=[[1, 128]], base=0,
                             channel_multiplier=-1,
                             allow_small_or_imprecise_dtypes=True), "c_id", 1)
            @blk.vector
            def _(v):
                b.wait(v, "c_id")
                b.inc(v.tensor_scalar(b.T["ident"][:, :], b.T["iota_t"][:, :], 0.0,
                                      None, ALU.is_equal), "c_id", 1)
        b.dout("dbg_bcast", [128, 16], F32)
        b.dout("dbg_prow", [1, 16], F32)
        b.dout("dbg_zsum", [128, DC], F32)
        b.dout("dbg_hdnT", [66, S], F32)
        b.dout("dbg_wdsel", [128, DC * 64], F32)
        b.dout("dbg_wupsel", [66, 1024], F32)
        b.dout("dbg_youts", [128, 2 * D], F32)
        b.dout("dbg_zT", [128, DC * S], F32)
        emit_stats_zT(b, "x1")
        emit_l1ffn(b, "f1_out", "dev")
        with nc.Block("l1f_dbg") as blk:
            @blk.gpsimd
            def _(g):
                b.wait(g, "f_out")
                b.dma(g, b.dram["dbg_bcast"].ap(), b.T["bcast"][:, :], "f_dbg")
                b.dma(g, b.dram["dbg_prow"].ap(), b.T["prow"][:, :], "f_dbg")
                b.dma(g, b.dram["dbg_zsum"].ap(), b.T["zsum"][:, :], "f_dbg")
                b.dma(g, b.dram["dbg_hdnT"].ap(), b.T["hdnT"][:, :], "f_dbg")
                b.dma(g, b.dram["dbg_wdsel"].ap(), b.T["wdsel"][:, :], "f_dbg")
                b.dma(g, b.dram["dbg_wupsel"].ap(), b.T["wupsel"][:, :], "f_dbg")
                b.dma(g, b.dram["dbg_youts"].ap(), b.T["youts"][:, :], "f_dbg")
                b.dma(g, b.dram["dbg_zT"].ap(), b.T["xT"][:, :], "f_dbg")
                g.wait_ge(b.get_sem("f_dbg"), b.cnt["f_dbg"])
    elif which == "l1a":
        b.din("x1", [S, D], F32)
        alloc_l1a(b)
        b.dout("a1_out", [S, D], F32)
        b.dout("dbg_nrm", [128, 16], F32)
        b.dout("dbg_hre", [128, 4 * 512], F32)
        b.dout("dbg_carry", [128, 16], F32)
        b.dout("dbg_a2", [128, 8 * 512], F32)
        b.dout("dbg_uv", [128, 8 * 512], F32)
        b.dout("dbg_Xnr", [128, 4 * 512], F32)
        b.dout("dbg_Xni", [128, 4 * 512], F32)
        emit_idsetup(b)
        emit_stats_zT(b, "x1", gains=False)
        emit_l1a(b, "a1_out", accumulate=False)
        with nc.Block("l1a_dbg") as blk:
            @blk.gpsimd
            def _(g):
                g.wait_ge(b.get_sem("l_out"), b.cnt["l_out"])
                b.dma(g, b.dram["dbg_nrm"].ap(), b.T["nrm"][:, :], "g_dbg")
                b.dma(g, b.dram["dbg_hre"].ap(), b.T["hre"][:, :], "g_dbg")
                b.dma(g, b.dram["dbg_carry"].ap(), b.T["carry"][:, :], "g_dbg")
                b.dma(g, b.dram["dbg_a2"].ap(), b.T["a2"][:, :], "g_dbg")
                b.dma(g, b.dram["dbg_uv"].ap(), b.T["uv"][:, :], "g_dbg")
                b.dma(g, b.dram["dbg_Xnr"].ap(), b.T["Xnr"][:, :], "g_dbg")
                b.dma(g, b.dram["dbg_Xni"].ap(), b.T["Xni"][:, :], "g_dbg")
                g.wait_ge(b.get_sem("g_dbg"), b.cnt["g_dbg"])
    elif which == "moe":
        alloc_moe(b)
        b.dout("f0_out", [S, D], F32)
        emit_p0(b)
        emit_gate(b)
        emit_rebuild_xhatT(b)
        emit_moe(b, "f0_out", accumulate=False)
        with nc.Block("moe_fin") as blk:
            @blk.gpsimd
            def _(g):
                g.wait_ge(b.get_sem("d_ydev"), b.cnt["d_ydev"])
                b.seminc(g, "all_done", 1)
    elif which == "c0":
        alloc_att(b)
        alloc_moe_chain(b)
        emit_p0(b)
        emit_gate(b)
        emit_l0attn(b, "x1dram", accumulate=True, wait_gate=True)
        emit_x0_reload(b)
        emit_rebuild_xhatT(b, wait_reload=True)
        emit_moe(b, "x1dram", accumulate=True)
        with nc.Block("c0_fin") as blk:
            @blk.gpsimd
            def _(g):
                g.wait_ge(b.get_sem("d_yout"), b.cnt["d_yout"])
                b.seminc(g, "all_done", 1)
    b.es.close()
    return b


def host_prep(inputs):
    f = np.ascontiguousarray
    de = inputs["L0_attn_dist_emb"]
    dist_pad = de[np.clip(np.arange(4095) - 1024, 0, 2046)]
    pr = {}
    pr["dist_padT"] = f(np.vstack([dist_pad.T, dist_pad.T]))
    pr["dist_padT_rev"] = f(np.vstack([dist_pad.T[:, ::-1], dist_pad.T[:, ::-1]]))
    perm = np.concatenate([np.arange(0, 2 * INNER, 2), np.arange(1, 2 * INNER, 2)])
    pr["w_aT_perm"] = f(inputs["L1_attn_w_a"][perm].T)
    pr["w_xT_perm"] = f(inputs["L1_attn_w_x"][perm].T)
    pr["w_outT"] = f(inputs["L1_attn_w_out"].T)
    pr["mat_v_T_re"] = f(inputs["L1_attn_mat_v_real"].T)
    pr["mat_v_T_im"] = f(inputs["L1_attn_mat_v_imag"].T)
    pr["b_downT"] = f(inputs["L1_ffn_b_down"].T)
    wg = inputs["L0_ffn_Wg"]
    pr["wg_cols"] = f(wg.reshape(DC, 128, E_MOE).transpose(1, 0, 2).reshape(128, DC * E_MOE))
    b1 = inputs["L0_ffn_b1"]
    pr["b1_cols"] = f(b1.reshape(E_MOE, 32, 128).transpose(2, 0, 1).reshape(128, E_MOE * 32))
    gam = inputs["L0_ffn_gamma"]
    pr["gam_cols"] = f(gam.reshape(E_MOE, DC, 128).transpose(2, 0, 1).reshape(128, E_MOE * DC))
    bet = inputs["L0_ffn_beta"]
    pr["bet_cols"] = f(bet.reshape(E_MOE, DC, 128).transpose(2, 0, 1).reshape(128, E_MOE * DC))
    pr["L0_ffn_W1"] = inputs["L0_ffn_W1"].reshape(-1)
    pr["L0_ffn_W2"] = inputs["L0_ffn_W2"].reshape(-1)
    cols = lambda v: f(v.reshape(DC, 128).T)
    pr["lng_cols"] = cols(inputs["L1_ffn_ln_g"])
    pr["lnb_cols"] = cols(inputs["L1_ffn_ln_b"])
    pr["bn_cols"] = f(np.concatenate([cols(inputs["L1_ffn_bn_g"]), cols(inputs["L1_ffn_bn_b"]),
                                      cols(inputs["L1_ffn_bn_mean"]), cols(inputs["L1_ffn_bn_var"])], 1))
    pr["wr_cols"] = f(inputs["L1_ffn_Wr"].reshape(DC, 128, 8).transpose(1, 0, 2).reshape(128, DC * 8))
    pr["br_row"] = f(inputs["L1_ffn_br"].reshape(1, 8))
    pr["wdn_cols"] = f(inputs["L1_ffn_W_down"].reshape(8, DC, 128, 64).transpose(2, 0, 1, 3).reshape(128, 8 * DC * 64))
    pr["wup_rows"] = f(inputs["L1_ffn_W_up"].transpose(1, 0, 2).reshape(64, 8 * 1024))
    pr["bq_cols"] = f(inputs["L0_attn_bq"].reshape(8, 128).T)
    pr["bk_cols"] = f(inputs["L0_attn_bk"].reshape(8, 128).T)
    pr["bv_cols"] = f(inputs["L0_attn_bv"].reshape(8, 128).T)
    pr["wo_rows"] = f(inputs["L0_attn_Wo"].reshape(DC, 128, 1024).transpose(1, 0, 2).reshape(128, DC * 1024))
    pr["gbo_rows"] = f(np.stack([inputs["L0_attn_ln_gamma"], inputs["L0_attn_ln_beta"],
                                 inputs["L0_attn_bo"]]))
    return pr


def _run(b, maps):
    return run_bass_kernel_spmd(b.nc, maps, core_ids=list(range(8)))


def kernel(**inputs):
    feed = {k: np.ascontiguousarray(np.asarray(v, np.float32))
            for k, v in inputs.items()}
    feed.update({k: np.ascontiguousarray(np.asarray(v, np.float32))
                 for k, v in host_prep(feed).items()})
    x0 = feed["L0_attn_hidden_states"]
    zer = np.zeros((S, D), np.float32)

    def maps_for(b, per_core):
        out = []
        for c in range(8):
            m = {}
            for name in b.input_names:
                if name in per_core:
                    m[name] = per_core[name][c]
                elif name == "x0":
                    m[name] = zer
                else:
                    m[name] = feed[name]
            out.append(m)
        return out

    b0 = build_dev("c0")
    r0 = _run(b0, maps_for(b0, {"x0": x0}))
    x1 = [np.ascontiguousarray(r0.results[c]["x1_out"]) for c in range(8)]

    b1 = build_dev("l1a")
    r1 = _run(b1, maps_for(b1, {"x1": x1}))
    a1 = [np.ascontiguousarray(r1.results[c]["a1_out"]) for c in range(8)]

    b2 = build_dev("l1f")
    r2 = _run(b2, maps_for(b2, {"x1": x1}))
    f1 = [np.ascontiguousarray(r2.results[c]["f1_out"]) for c in range(8)]

    b3 = build_dev("sum")
    r3 = _run(b3, maps_for(b3, {"sa": x1, "sb": a1, "sc": f1}))
    return np.stack([r3.results[c]["sum_out"] for c in range(8)]).astype(np.float32)
